# version 39
# baseline (speedup 1.0000x reference)
import math
import numpy as np
import ml_dtypes
import contextlib
import concourse.bass as bass
import concourse.mybir as mybir
from concourse.bass_utils import run_bass_kernel_spmd

F32 = mybir.dt.float32
BF16 = mybir.dt.bfloat16
AF = mybir.ActivationFunctionType
ALU = mybir.AluOpType
AX = mybir.AxisListType
BF = ml_dtypes.bfloat16

N_CORES = 8
DEBUG_STOP = None
D = 1024
NCH = 8
NHEAD = 16
HD = 64
LAT = 32
ISQL = 1.0 / math.sqrt(LAT)

PAIR_MASK = []
for _g in range(16):
    PAIR_MASK += [2 * _g + 1, 2 * _g]


class Res:
    __slots__ = ("w", "r", "bank")

    def __init__(self, bank=False):
        self.w = None
        self.r = {}
        self.bank = bank


class Prog:
    ENGINES = ("sync", "vector", "tensor", "scalar", "gpsimd")
    ESEM = dict(sync="LDI0", vector="DV", tensor="PE", scalar="AC", gpsimd="PL")
    CONS_DMA = ("LDI0", "LDI1", "LDI2", "LDI3", "LDI4", "LDI5",
                "ST0", "ST1", "ST2", "ST3", "ST4", "ST5", "ST6", "ST7",
                "PF0", "PF1", "PF2", "PF3", "PF4", "PF5", "PF6", "PF7")
    SEM_NAMES = ("LDI0", "LDI1", "LDI2", "LDI3", "LDI4", "LDI5",
                 "ST0", "ST1", "ST2", "ST3", "ST4", "ST5", "ST6", "ST7",
                 "PF0", "PF1", "PF2", "PF3", "PF4", "PF5", "PF6", "PF7",
                 "PE", "DV", "AC", "PL",
                 "SL0", "SL1", "SL2", "SL3", "SL4", "SL5", "SL6")

    def __init__(self):
        self.streams = {e: [] for e in self.ENGINES}
        self.cnt = {s: 0 for s in self.SEM_NAMES}
        self.lastw = {}
        self.res = {}
        self.ldi_ctr = 0
        self.st_ctr = 0
        self.pf_ctr = 0

    def R(self, *key):
        kind = key[0]
        return self.res.setdefault(key, Res(bank=(kind == "bank")))

    def _wait(self, eng, tok):
        if tok is None:
            return
        sem, val = tok

        k = (eng, sem)
        if self.lastw.get(k, 0) >= val:
            return
        self.lastw[k] = val
        self.streams[eng].append(("w", sem, val))

    def _pre(self, eng, reads, writes):
        own = self.ESEM[eng]
        for r in reads:
            self._wait(eng, r.w)
            if r.bank:
                for s, v in r.r.items():
                    if s != own:
                        self._wait(eng, (s, v))
        for w in writes:
            self._wait(eng, w.w)
            for s, v in w.r.items():
                self._wait(eng, (s, v))

    def _post(self, tok, reads, writes):
        for r in reads:
            r.r[tok[0]] = max(r.r.get(tok[0], 0), tok[1])
        for w in writes:
            w.w = tok
            w.r = {}

    def op(self, eng, fn, reads=(), writes=(), dma=None):
        self._pre(eng, reads, writes)
        if dma:
            if dma == "LDI":
                dma = "LDI%d" % (self.ldi_ctr % 6)
                self.ldi_ctr += 1
                if self.cnt[dma]:
                    self._wait(eng, (dma, self.cnt[dma]))
            elif dma == "PF":
                dma = "PF%d" % (self.pf_ctr % 8)
                self.pf_ctr += 1
                if self.cnt[dma]:
                    self._wait(eng, (dma, self.cnt[dma]))
            elif dma == "ST":
                dma = "ST%d" % (self.st_ctr % 8)
                self.st_ctr += 1
                if self.cnt[dma]:
                    self._wait(eng, (dma, self.cnt[dma]))
            sem = dma
            inc = 16
        else:
            sem = self.ESEM[eng]
            inc = 1
        self.cnt[sem] += inc
        tok = (sem, self.cnt[sem])
        self.streams[eng].append(("i", fn, sem, inc))
        self._post(tok, reads, writes)
        return tok

    def group(self, eng, fns, reads=(), writes=(), fn_reads=None):
        self._pre(eng, reads, writes)
        sem = self.ESEM[eng]
        for i, f in enumerate(fns):
            if fn_reads is not None:
                for r in fn_reads[i]:
                    self._wait(eng, r.w)
            last = (i == len(fns) - 1)
            if last:
                self.cnt[sem] += 1
                self.streams[eng].append(("i", f, sem, 1))
            else:
                self.streams[eng].append(("i", f, None, 0))
        tok = (sem, self.cnt[sem])
        all_reads = list(reads)
        if fn_reads is not None:
            for fr in fn_reads:
                all_reads += list(fr)
        self._post(tok, all_reads, writes)
        return tok

    def raw_wait(self, eng, sem, val):
        self.streams[eng].append(("w", sem, val))


def _host_consts(seq):
    a = np.arange(0, LAT, 2, dtype=np.float32) / np.float32(LAT)
    inv = (np.float32(1.0) / (np.float32(10000.0) ** a)).astype(np.float32)
    t = np.arange(seq, dtype=np.float32)
    freqs = np.outer(t, inv).astype(np.float32)
    cos = np.repeat(np.cos(freqs), 2, axis=1)[:, :LAT]
    sin = np.repeat(np.sin(freqs), 2, axis=1)[:, :LAT]
    sgn = np.where(np.arange(LAT) % 2 == 0, -1.0, 1.0).astype(np.float32)
    sin_s = sin * sgn[None, :]
    cos4 = np.ascontiguousarray(np.tile(cos.T, (4, 1)))
    sin4 = np.ascontiguousarray(np.tile(sin_s.T, (4, 1)))
    causal = (np.arange(128)[:, None] <= np.arange(128)[None, :]).astype(np.float32)
    maskbig = np.where(np.arange(128)[:, None] > np.arange(128)[None, :],
                       np.float32(-1e30), np.float32(0.0))
    id2 = np.vstack([np.eye(64, dtype=np.float32)] * 2)
    onesblk = np.zeros((128, 4), np.float32)
    for g in range(4):
        onesblk[32 * g:32 * g + 32, g] = 1.0
    return dict(
        cos4=cos4.astype(BF), sin4=sin4.astype(BF),
        causal=causal.astype(BF), id2=id2.astype(BF),
        id128=np.eye(128, dtype=np.float32).astype(BF),
        maskbig=maskbig.astype(BF),
        ones128=np.ones((128, 1), np.float32).astype(BF),
        ones1x=np.ones((1, 128), np.float32).astype(BF),
        onesblk=onesblk.astype(BF),
    )


def _finish(nc, p, sems, st):
    def replay(stream):
        def body(eng):
            for e in stream:
                if e[0] == "w":
                    eng.wait_ge(sems[e[1]], e[2])
                else:
                    ins = e[1](eng)
                    if e[2] is not None:
                        ins.then_inc(sems[e[2]], e[3])
        return body
    with nc.Block() as blk:
        blk.sync(replay(p.streams["sync"]))
        blk.vector(replay(p.streams["vector"]))
        blk.tensor(replay(p.streams["tensor"]))
        blk.scalar(replay(p.streams["scalar"]))
        blk.gpsimd(replay(p.streams["gpsimd"]))
    st.__exit__(None, None, None)
    return nc


def build_program(bpc, seq):
    HALF = seq // 2
    TTA = min(512, HALF)
    NT = HALF // TTA
    CH = HALF // 128
    CHB = seq // 128
    NTB = seq // TTA
    QW = HALF

    nc = bass.Bass("TRN2", target_bir_lowering=False, debug=False)

    def dram(name, shape, dt=F32, kind="ExternalInput"):
        return nc.dram_tensor(name, shape, dt, kind=kind).ap()

    d_x = dram("x", [bpc, D, seq])
    d_out = dram("out", [bpc, D, seq], kind="ExternalOutput")
    d_wattn = dram("w_attn", [3 * D, D])
    d_ba48 = dram("b_attn48", [64, 48])
    d_q2lT = dram("q2l_wT", [64, 32])
    d_q2lb = dram("q2l_b", [32, 1])
    d_kv2lw = dram("kv2l_w", [32, 64])
    d_kv2lT = dram("kv2l_wT", [64, 32])
    d_kv2lb = dram("kv2l_b", [32, 1])
    d_l2oT = dram("l2o_wT", [32, 64])
    d_tl2o8 = dram("tl2o8", [128, 8])
    d_wqkT4 = dram("wqk_wT4", [128, 32])
    d_wqkb4 = dram("wqk_b4", [128, 1])
    d_cpT = dram("cprojT", [D, D])
    d_cpb8 = dram("cproj_b8", [128, 8])
    d_ffnT = dram("ffnT", [D, D])
    d_disha2 = dram("disha2", [128, D])
    d_cos4 = dram("cos4", [128, seq], BF16)
    d_sin4 = dram("sin4", [128, seq], BF16)
    d_causal = dram("causal", [128, 128], BF16)
    d_id128 = dram("id128", [128, 128], BF16)
    d_maskbig = dram("maskbig", [128, 128], BF16)
    d_id2 = dram("id2", [128, 64], BF16)
    d_ones128 = dram("ones128", [128, 1], BF16)
    d_ones1x = dram("ones1x", [1, 128], BF16)
    d_onesblk = dram("onesblk", [128, 4], BF16)

    wa_strm = dram("wa_strm", [16, 128, 8 * 128], BF16, kind="Internal")
    me_cmaj = dram("me_cmaj", [8, 128, 8 * 128], BF16, kind="Internal")
    me_strm = dram("me_strm", [8, 128, 8 * 128], BF16, kind="Internal")
    cp_cmaj = dram("cp_cmaj", [8, 128, 8 * 128], BF16, kind="Internal")
    cp_strm = dram("cp_strm", [8, 128, 8 * 128], BF16, kind="Internal")

    st = contextlib.ExitStack()
    st.__enter__()

    def sb(name, shape, dt):
        return st.enter_context(nc.sbuf_tensor(name, shape, dt))

    x_sb = sb("x_sb", [128, 8 * HALF], F32)
    xb_sb = sb("xb_sb", [128, 8 * TTA], BF16)
    sq_sb = sb("sq_sb", [128, 2 * TTA], BF16)
    ln_sb = sb("ln_sb", [128, 8 * TTA], BF16)
    mb_sb = sb("mb_sb", [128, TTA], BF16)
    ab_sb = sb("ab_sb", [128, TTA], BF16)
    m_f = sb("m_f", [1, TTA], F32)
    m_bf = sb("m_bf", [1, TTA], BF16)
    msq = sb("msq", [1, TTA], F32)
    var_f = sb("var_f", [1, TTA], F32)
    a_f = sb("a_f", [1, TTA], F32)
    a_bf = sb("a_bf", [1, TTA], BF16)
    qrot = sb("qrot", [128, TTA], F32)
    t1_sb = sb("t1_sb", [128, TTA], BF16)
    t2_sb = sb("t2_sb", [128, TTA], BF16)
    qp_sb = sb("qp_sb", [128, 2 * TTA], BF16)
    v_sb = sb("v_sb", [128, 2 * TTA], BF16)
    sq2_sb = sb("sq2_sb", [128, 2 * TTA], BF16)
    q_store = sb("q_store", [128, 4 * seq], BF16)
    k_store = sb("k_store", [128, 4 * seq], BF16)
    v_store = sb("v_store", [128, CHB * 1040], BF16)
    qn_acc = sb("qn_acc", [4, 4 * NTB], F32)
    kn_acc = sb("kn_acc", [4, 4 * NTB], F32)
    qmx = sb("qmx", [4, 1], F32)
    kmx = sb("kmx", [4, 1], F32)
    prod4 = sb("prod4", [4, 1], F32)
    g14 = sb("g14", [1, 4], F32)
    mmax = sb("mmax", [1, 1], F32)
    msqrt = sb("msqrt", [1, 1], F32)
    mneg_bf = sb("mneg_bf", [1, 1], BF16)
    b_exp = sb("b_exp", [128, 1], F32)
    PT = sb("PT", [128, 4 * QW], BF16)
    y_sb = sb("y_sb", [128, 8 * QW], BF16)
    r_sb = sb("r_sb", [1, 2 * QW], F32)
    r_bf = sb("r_bf", [1, 2 * QW], BF16)
    rb_sb = sb("rb_sb", [64, 2 * QW], BF16)
    NSLOT = 7
    wslot = sb("wslot", [128, NSLOT * 1024], BF16)
    stg = sb("stg", [128, 1024], F32)
    build_sb = sb("build_sb", [128, 1024], BF16)
    slab1 = sb("slab1", [64, 1024], BF16)
    ba_sb = sb("ba_sb", [64, 48], F32)
    ba_bf = sb("ba_bf", [64, 48], BF16)
    tq2lT = sb("tq2lT", [64, 32], F32)
    tkv2lT = sb("tkv2lT", [64, 32], F32)
    q2lb_sb = sb("q2lb_sb", [32, 1], F32)
    kv2lb_sb = sb("kv2lb_sb", [32, 1], F32)
    kvb_bf = sb("kvb_bf", [32, 1], BF16)
    tl2oT = sb("tl2oT", [32, 64], F32)
    tkv2lw = sb("tkv2lw", [32, 64], F32)
    twqk = sb("twqk", [128, 32], F32)
    wqkb_sb = sb("wqkb_sb", [128, 1], F32)
    b_wqks = sb("b_wqks", [128, 1], F32)
    b_q2lT = sb("b_q2lT", [64, 32], BF16)
    b_kv2lT = sb("b_kv2lT", [64, 32], BF16)
    l2oT_bf = sb("l2oT_bf", [32, 64], BF16)
    kv2lw_bf = sb("kv2lw_bf", [32, 64], BF16)
    wqk4_bf = sb("wqk4_bf", [128, 32], BF16)
    AT_bf = sb("AT_bf", [64, 64], BF16)
    b_qlat = sb("b_qlat", [128, 4], F32)
    b_qlat_r = sb("b_qlat_r", [128, 4], F32)
    b_klat = sb("b_klat", [128, 4], F32)
    b_klat_r = sb("b_klat_r", [128, 4], F32)
    b_vo2 = sb("b_vo2", [128, 1], F32)
    cos_bf = sb("cos_bf", [128, HALF], BF16)
    sin_bf = sb("sin_bf", [128, HALF], BF16)
    causal_sb = sb("causal_sb", [128, 128], BF16)
    id128_sb = sb("id128_sb", [128, 128], BF16)
    id2_sb = sb("id2_sb", [128, 64], BF16)
    ones128_sb = sb("ones128_sb", [128, 1], BF16)
    ones1x_sb = sb("ones1x_sb", [1, 128], BF16)
    onesblk_sb = sb("onesblk_sb", [128, 4], BF16)
    c_eff = sb("c_eff", [128, 8], F32)
    cb_sb = sb("cb_sb", [128, 8], F32)
    tl2o_sb = sb("tl2o_sb", [128, 8], F32)
    tl2o_bf = sb("tl2o_bf", [128, 8], BF16)
    eps_sb = sb("eps_sb", [1, 1], F32)

    disha_sb = ln_sb
    out_stg = stg
    PS = st.enter_context(nc.psum_tensor("PS", [128, 4096], F32))
    PSB = PS.bitcast(BF16)

    sems = {}
    for s in Prog.SEM_NAMES:
        sems[s] = st.enter_context(nc.semaphore(s))

    p = Prog()
    BK = [p.R("bank", i) for i in range(8)]

    def bank_ap(i, lo, hi):
        return PS[:, 512 * i + lo:512 * i + hi]

    slot_ctr = [0]

    def stream_slot(src_ap, src_res):
        j = slot_ctr[0] % NSLOT
        slot_ctr[0] += 1
        r = p.R("slot", j)
        p.op("sync", lambda eng, s=src_ap, j=j: eng.dma_start(
            out=wslot[:, 1024 * j:1024 * j + 1024], in_=s),
            reads=[src_res], writes=[r], dma="SL%d" % j)
        return j, r

    def slot_ap(j, d):
        return wslot[:, 1024 * j + 128 * d:1024 * j + 128 * d + 128]

    def ld(dst_ap, src_ap, res):
        p.op("sync", lambda eng, d=dst_ap, s=src_ap: eng.dma_start(out=d, in_=s),
             writes=[res], dma="LDI")

    for o in range(NCH):
        p.op("sync", lambda eng, o=o: eng.dma_start(
            out=x_sb[:, HALF * o:HALF * o + HALF],
            in_=d_x[0, 128 * o:128 * o + 128, 0:HALF]),
            writes=[p.R("x", o, tt) for tt in range(NT)], dma="LDI")

    ld(ba_sb[:], d_ba48, p.R("ba"))
    ld(tq2lT[:], d_q2lT, p.R("tq2lT"))
    ld(tkv2lT[:], d_kv2lT, p.R("tkv2lT"))
    ld(q2lb_sb[:], d_q2lb, p.R("q2lb"))
    ld(kv2lb_sb[:], d_kv2lb, p.R("kv2lb"))
    ld(tl2oT[:], d_l2oT, p.R("tl2oT"))
    ld(tkv2lw[:], d_kv2lw, p.R("tkv2lw"))
    ld(twqk[:], d_wqkT4, p.R("twqk"))
    ld(wqkb_sb[:], d_wqkb4, p.R("wqkb"))
    ld(cb_sb[:], d_cpb8, p.R("cb"))
    ld(tl2o_sb[:], d_tl2o8, p.R("tl2o"))
    ld(stg[:, 0:1024], d_disha2, p.R("stg"))
    p.op("vector", lambda eng: eng.tensor_copy(disha_sb[:, 0:1024], stg[:, 0:1024]),
         reads=[p.R("stg")], writes=[p.R("disha")])
    ld(causal_sb[:], d_causal, p.R("causal"))
    ld(id128_sb[:], d_id128, p.R("id128"))
    p.op("vector", lambda eng: eng.memset(eps_sb[:], 1e-5), writes=[p.R("eps")])
    ld(id2_sb[:], d_id2, p.R("id2"))
    ld(ones128_sb[:], d_ones128, p.R("ones128"))
    ld(ones1x_sb[:], d_ones1x, p.R("ones1x"))
    ld(onesblk_sb[:], d_onesblk, p.R("onesblk"))

    def cp32(dst_lo, dst_hi, src_lo, src_hi, tensor_d, tensor_s, rres, wres):
        for i in range((dst_hi - dst_lo) // 32):
            p.op("vector", lambda eng, a=dst_lo + 32 * i, b=src_lo + 32 * i,
                 td=tensor_d, tsr=tensor_s: eng.tensor_copy(
                     td[a:a + 32, :], tsr[b:b + 32, :]),
                 reads=[rres], writes=[wres])

    p.op("vector", lambda eng: eng.tensor_copy(ba_bf[:], ba_sb[:]),
         reads=[p.R("ba")], writes=[p.R("babf")])
    p.op("vector", lambda eng: eng.tensor_copy(b_q2lT[:], tq2lT[:]),
         reads=[p.R("tq2lT")], writes=[p.R("q2lT2")])
    p.op("vector", lambda eng: eng.tensor_copy(b_kv2lT[:], tkv2lT[:]),
         reads=[p.R("tkv2lT")], writes=[p.R("kv2lT2")])
    p.op("vector", lambda eng: eng.tensor_copy(l2oT_bf[:], tl2oT[:]),
         reads=[p.R("tl2oT")], writes=[p.R("l2oTbf")])
    p.op("vector", lambda eng: eng.tensor_copy(kv2lw_bf[:], tkv2lw[:]),
         reads=[p.R("tkv2lw")], writes=[p.R("kv2lwbf")])
    p.op("vector", lambda eng: eng.tensor_copy(wqk4_bf[:], twqk[:]),
         reads=[p.R("twqk")], writes=[p.R("wqk4bf")])
    p.op("vector", lambda eng: eng.tensor_copy(kvb_bf[:], kv2lb_sb[:]),
         reads=[p.R("kv2lb")], writes=[p.R("kvbbf")])
    p.op("vector", lambda eng: eng.tensor_copy(tl2o_bf[:], tl2o_sb[:]),
         reads=[p.R("tl2o")], writes=[p.R("tl2obf")])
    p.op("scalar", lambda eng: eng.mul(b_wqks[:], wqkb_sb[:], ISQL),
         reads=[p.R("wqkb")], writes=[p.R("bwqks")])

    p.group("tensor", [lambda eng: eng.matmul(
        bank_ap(1, 0, 64)[0:64, :], kv2lw_bf[:], l2oT_bf[:],
        start=True, stop=True)],
        reads=[p.R("kv2lwbf"), p.R("l2oTbf")], writes=[BK[1]])
    p.op("vector", lambda eng: eng.tensor_copy(AT_bf[:], bank_ap(1, 0, 64)[0:64, :]),
         reads=[BK[1]], writes=[p.R("AT2")])

    p.group("tensor", [lambda eng: eng.matmul(
        bank_ap(2, 0, 1)[0:64, :], l2oT_bf[:], kvb_bf[:], start=True, stop=True)],
        reads=[p.R("l2oTbf"), p.R("kvbbf")], writes=[BK[2]])
    p.op("vector", lambda eng: eng.tensor_copy(b_vo2[0:64, :], bank_ap(2, 0, 1)[0:64, :]),
         reads=[BK[2]], writes=[p.R("bvo")])
    cp32(64, 128, 0, 64, b_vo2, b_vo2, p.R("bvo"), p.R("bvo"))

    if DEBUG_STOP == "init_smalls":
        return _finish(nc, p, sems, st)
    def fold_m(m):
        if m < 8:
            qk = m < 4
            hg = m % 4
            rhsT = b_q2lT if qk else b_kv2lT
            rhs_res = p.R("q2lT2") if qk else p.R("kv2lT2")
            brow = 0 if qk else D
            for hh in range(4):
                r0 = brow + 256 * hg + 64 * hh
                ld(stg[0:64, 0:1024], d_wattn[r0:r0 + 64, :], p.R("stg"))
                p.op("vector", lambda eng: eng.tensor_copy(
                    slab1[:], stg[0:64, 0:1024]),
                    reads=[p.R("stg")], writes=[p.R("slab", 0)])
                fns = []
                for c in range(8):
                    bk = 1 if c < 4 else 2
                    col0 = 512 * bk + 128 * (c % 4)
                    fns.append(lambda eng, hh=hh, c=c, col0=col0, rhsT=rhsT:
                               eng.matmul(
                        PS[:, col0 + 32 * hh:col0 + 32 * hh + 32],
                        slab1[:, 128 * c:128 * c + 128],
                        rhsT[:], start=True, stop=True))
                p.group("tensor", fns,
                        reads=[p.R("slab", 0), rhs_res],
                        writes=[BK[1], BK[2]])
            bdst = b_qlat if qk else b_klat
            bres = p.R("bql") if qk else p.R("bkl")
            pb_sb = q2lb_sb if qk else kv2lb_sb
            pb_res = p.R("q2lb") if qk else p.R("kv2lb")
            for hh in range(4):
                h = 4 * hg + hh
                fl = brow // 64 + h
                p.group("tensor", [lambda eng, fl=fl, rhsT=rhsT: eng.matmul(
                    bank_ap(3, 0, 1)[0:32, :], rhsT[:],
                    ba_bf[:, fl:fl + 1], start=True, stop=True)],
                    reads=[rhs_res, p.R("babf")], writes=[BK[3]])
                p.op("vector", lambda eng, hh=hh, hg=hg, bdst=bdst, pb=pb_sb:
                     eng.tensor_scalar(
                         bdst[32 * hh:32 * hh + 32, hg:hg + 1],
                         bank_ap(3, 0, 1)[0:32, :], pb[:, 0:1], None, ALU.add),
                     reads=[BK[3], pb_res], writes=[bres])
        else:
            hp = m - 8
            for hh in range(2):
                r0 = 2 * D + 128 * hp + 64 * hh
                ld(stg[0:64, 0:1024], d_wattn[r0:r0 + 64, :], p.R("stg"))
                p.op("vector", lambda eng: eng.tensor_copy(
                    slab1[:], stg[0:64, 0:1024]),
                    reads=[p.R("stg")], writes=[p.R("slab", 0)])
                fns = []
                for c in range(8):
                    bk = 1 if c < 4 else 2
                    col0 = 512 * bk + 128 * (c % 4)
                    fns.append(lambda eng, hh=hh, c=c, col0=col0: eng.matmul(
                        PS[:, col0 + 64 * hh:col0 + 64 * hh + 64],
                        slab1[:, 128 * c:128 * c + 128],
                        AT_bf[:], start=True, stop=True))
                p.group("tensor", fns,
                        reads=[p.R("slab", 0), p.R("AT2")],
                        writes=[BK[1], BK[2]])
        for half in range(2):
            bk = 1 + half
            p.op("scalar", lambda eng, half=half, bk=bk: eng.activation(
                build_sb[:, 512 * half:512 * half + 512],
                bank_ap(bk, 0, 512), AF.Copy),
                reads=[BK[bk]], writes=[p.R("build")])
        p.op("sync", lambda eng, m=m: eng.dma_start(
            out=wa_strm[m, :, :], in_=build_sb[:]),
            reads=[p.R("build")], writes=[p.R("wa_strm", m)], dma="LDI")

    nfold = 16 if DEBUG_STOP not in ("init_fold1",) else 1
    for m in range(nfold):
        fold_m(m)
    if DEBUG_STOP in ("init_folds", "init_fold1"):
        return _finish(nc, p, sems, st)

    p.op("vector", lambda eng: eng.stream_shuffle(b_qlat_r[:], b_qlat[:], PAIR_MASK),
         reads=[p.R("bql")], writes=[p.R("bqlr")])
    p.op("vector", lambda eng: eng.stream_shuffle(b_klat_r[:], b_klat[:], PAIR_MASK),
         reads=[p.R("bkl")], writes=[p.R("bklr")])

    if DEBUG_STOP == "init_rope":
        return _finish(nc, p, sems, st)
    for c in range(8):
        ld(stg[:, 0:1024], d_ffnT[128 * c:128 * c + 128, :], p.R("stg"))
        p.op("vector", lambda eng: eng.tensor_tensor(
            build_sb[:], stg[:, 0:1024], disha_sb[:, 0:1024], ALU.add),
            reads=[p.R("stg"), p.R("disha")], writes=[p.R("build")])
        p.op("sync", lambda eng, c=c: eng.dma_start(
            out=me_cmaj[c, :, :], in_=build_sb[:]),
            reads=[p.R("build")], writes=[p.R("me_cmaj", c)], dma="LDI")
    for c in range(8):
        ld(stg[:, 0:1024], d_cpT[128 * c:128 * c + 128, :], p.R("stg"))
        p.op("vector", lambda eng: eng.tensor_copy(build_sb[:], stg[:, 0:1024]),
             reads=[p.R("stg")], writes=[p.R("build")])
        p.op("sync", lambda eng, c=c: eng.dma_start(
            out=cp_cmaj[c, :, :], in_=build_sb[:]),
            reads=[p.R("build")], writes=[p.R("cp_cmaj", c)], dma="LDI")

    if DEBUG_STOP == "init_mecp":
        return _finish(nc, p, sems, st)
    for o in range(8):
        p.op("sync", lambda eng, o=o: eng.dma_start(
            out=me_strm[o, :, :].rearrange("p (c j) -> c p j", c=8),
            in_=me_cmaj[:, :, 128 * o:128 * o + 128]),
            reads=[p.R("me_cmaj", c) for c in range(8)],
            writes=[p.R("me_strm", o)], dma="LDI")
        p.op("sync", lambda eng, o=o: eng.dma_start(
            out=cp_strm[o, :, :].rearrange("p (c j) -> c p j", c=8),
            in_=cp_cmaj[:, :, 128 * o:128 * o + 128]),
            reads=[p.R("cp_cmaj", c) for c in range(8)],
            writes=[p.R("cp_strm", o)], dma="LDI")

    def x_ap(o, hb_local_cols):
        lo, hi = hb_local_cols
        return x_sb[:, HALF * o + lo:HALF * o + hi]

    def ln_stats_chunk(tt, c, sqb=6):
        g0 = TTA * tt
        par = c % 2
        xres = p.R("x", c, tt)
        p.op("gpsimd", lambda eng, c=c, g0=g0: eng.tensor_copy(
            xb_sb[:, TTA * c:TTA * c + TTA], x_ap(c, (g0, g0 + TTA))),
            reads=[xres], writes=[p.R("xb", c)])
        p.op("gpsimd", lambda eng, c=c, par=par: eng.tensor_tensor(
            sq_sb[:, TTA * par:TTA * par + TTA],
            xb_sb[:, TTA * c:TTA * c + TTA],
            xb_sb[:, TTA * c:TTA * c + TTA], ALU.mult),
            reads=[p.R("xb", c)], writes=[p.R("sq", par)])
        p.group("tensor", [lambda eng, c=c: eng.matmul(
            bank_ap(0, 0, TTA)[0:1, :], ones128_sb[:],
            xb_sb[:, TTA * c:TTA * c + TTA],
            start=(c == 0), stop=(c == NCH - 1))],
            reads=[p.R("xb", c), p.R("ones128")], writes=[BK[0]])
        p.group("tensor", [lambda eng, c=c, par=par, sqb=sqb: eng.matmul(
            bank_ap(sqb, 0, TTA)[0:1, :], ones128_sb[:],
            sq_sb[:, TTA * par:TTA * par + TTA],
            start=(c == 0), stop=(c == NCH - 1))],
            reads=[p.R("sq", par), p.R("ones128")], writes=[BK[sqb]])

    def ln_finish(b, hb, tt, sqb=6):
        p.op("scalar", lambda eng: eng.activation(
            m_bf[:], bank_ap(0, 0, TTA)[0:1, :], AF.Copy, scale=1.0 / D),
            reads=[BK[0]], writes=[p.R("m_bf")])
        p.op("vector", lambda eng: eng.tensor_tensor(
            msq[:], m_bf[:], m_bf[:], ALU.mult),
            reads=[p.R("m_bf")], writes=[p.R("msq")])
        p.op("vector", lambda eng, sqb=sqb: eng.scalar_tensor_tensor(
            var_f[:], bank_ap(sqb, 0, TTA)[0:1, :], 1.0 / D, msq[:],
            ALU.mult, ALU.subtract),
            reads=[BK[sqb], p.R("msq")], writes=[p.R("var")])
        p.op("scalar", lambda eng: eng.activation(
            var_f[:], var_f[:], AF.Ln, bias=eps_sb[0:1, 0:1]),
            reads=[p.R("var"), p.R("eps")], writes=[p.R("var")])
        p.op("scalar", lambda eng: eng.activation(
            a_bf[:], var_f[:], AF.Exp, scale=-0.5),
            reads=[p.R("var")], writes=[p.R("a_bf")])
        p.group("tensor", [
            lambda eng: eng.matmul(bank_ap(6, 0, TTA), ones1x_sb[:], m_bf[:],
                                   start=True, stop=True),
            lambda eng: eng.matmul(bank_ap(7, 0, TTA), ones1x_sb[:], a_bf[:],
                                   start=True, stop=True)],
            reads=[p.R("ones1x")], writes=[BK[6], BK[7]],
            fn_reads=[[p.R("m_bf")], [p.R("a_bf")]])
        p.op("vector", lambda eng: eng.tensor_copy(mb_sb[:], bank_ap(6, 0, TTA)),
             reads=[BK[6]], writes=[p.R("mb")])
        p.op("vector", lambda eng: eng.tensor_copy(ab_sb[:], bank_ap(7, 0, TTA)),
             reads=[BK[7]], writes=[p.R("ab")])
        for c in range(NCH):
            p.op("vector", lambda eng, c=c: eng.tensor_tensor(
                ln_sb[:, TTA * c:TTA * c + TTA],
                xb_sb[:, TTA * c:TTA * c + TTA], mb_sb[:], ALU.subtract),
                reads=[p.R("xb", c), p.R("mb")], writes=[p.R("ln", c)])
            p.op("vector", lambda eng, c=c: eng.tensor_tensor(
                ln_sb[:, TTA * c:TTA * c + TTA],
                ln_sb[:, TTA * c:TTA * c + TTA], ab_sb[:], ALU.mult),
                reads=[p.R("ab"), p.R("ln", c)], writes=[p.R("ln", c)])

    ost_ctr = [0]

    def emit_ffn(b, hb, tt, ln_next=None, lnbank=6, store=False, nxt=None):
        g0 = TTA * tt
        for o in range(NCH):
            j, sres = stream_slot(me_strm[o, :, :], p.R("me_strm", o))
            bk = 4 + o % 2
            fns = []
            for dd in range(NCH):
                fns.append(lambda eng, j=j, dd=dd, bk=bk: eng.matmul(
                    bank_ap(bk, 0, TTA), slot_ap(j, dd),
                    ln_sb[:, TTA * dd:TTA * dd + TTA],
                    start=(dd == 0), stop=(dd == NCH - 1)))
            p.group("tensor", fns, reads=[sres], writes=[BK[bk]],
                    fn_reads=[[p.R("ln", dd)] for dd in range(NCH)])
            xr = p.R("x", o, tt)
            if store:
                j2 = ost_ctr[0] % 2
                ost_ctr[0] += 1
                p.op("vector", lambda eng, o=o, g0=g0, bk=bk, j2=j2:
                     eng.scalar_tensor_tensor(
                         out_stg[:, TTA * j2:TTA * j2 + TTA],
                         bank_ap(bk, 0, TTA), 0.5,
                         x_ap(o, (g0, g0 + TTA)), ALU.mult, ALU.add),
                     reads=[BK[bk], xr], writes=[p.R("ostg", j2)])
                p.op("scalar", lambda eng, o=o, g0=g0, b=b, hb=hb, j2=j2:
                     eng.dma_start(
                         out=d_out[b, 128 * o:128 * o + 128,
                                   HALF * hb + g0:HALF * hb + g0 + TTA],
                         in_=out_stg[:, TTA * j2:TTA * j2 + TTA]),
                     reads=[p.R("ostg", j2)],
                     writes=[p.R("outdram", b, hb, o, tt)], dma="ST")
                if nxt is not None:
                    if tt == NT - 1 and o >= 3:
                        ln_stats_chunk(0, o - 3, sqb=3)
                    nb2, nhb2 = nxt
                    p.op("scalar", lambda eng, b2=nb2, hb2=nhb2, o=o, g0=g0:
                         eng.dma_start(
                             out=x_sb[:, HALF * o + g0:HALF * o + g0 + TTA],
                             in_=d_x[b2, 128 * o:128 * o + 128,
                                     HALF * hb2 + g0:HALF * hb2 + g0 + TTA]),
                         writes=[p.R("x", o, tt)],
                         dma="PF")
            else:
                p.op("vector", lambda eng, o=o, g0=g0, bk=bk:
                     eng.scalar_tensor_tensor(
                         x_ap(o, (g0, g0 + TTA)), bank_ap(bk, 0, TTA), 0.5,
                         x_ap(o, (g0, g0 + TTA)), ALU.mult, ALU.add),
                     reads=[BK[bk], xr], writes=[xr])
            if ln_next is not None and o >= 3:
                ln_stats_chunk(ln_next, o - 3, sqb=lnbank)
        if ln_next is not None:
            ln_stats_chunk(ln_next, 5, sqb=lnbank)
            ln_stats_chunk(ln_next, 6, sqb=lnbank)
            ln_stats_chunk(ln_next, 7, sqb=lnbank)
        if store and nxt is not None and tt == NT - 1:
            ln_stats_chunk(0, 5, sqb=3)
            ln_stats_chunk(0, 6, sqb=3)
            ln_stats_chunk(0, 7, sqb=3)

    def emit_cattn(b, hb, tt, ln_next=None):
        g0 = TTA * tt
        gt0 = HALF * hb + g0
        ta = (HALF * hb + g0) // TTA
        pending = []

        def defer(due, fn):
            pending.append((due, fn))

        def flush(m=99):
            while pending and pending[0][0] <= m:
                pending.pop(0)[1]()

        for m in range(16):
            if ln_next is not None and 8 <= m:
                ln_stats_chunk(ln_next, m - 8)
            j, sres = stream_slot(wa_strm[m, :, :], p.R("wa_strm", m))
            bk = 1 + m % 3
            nb = 4 + m % 2
            fns = []
            for dd in range(NCH):
                fns.append(lambda eng, j=j, dd=dd, bk=bk: eng.matmul(
                    bank_ap(bk, 0, TTA), slot_ap(j, dd),
                    ln_sb[:, TTA * dd:TTA * dd + TTA],
                    start=(dd == 0), stop=(dd == NCH - 1)))
            p.group("tensor", fns, reads=[sres], writes=[BK[bk]],
                    fn_reads=[[p.R("ln", dd)] for dd in range(NCH)])
            flush(m)
            if m < 8:
                qk = m < 4
                hg = m % 4
                bias = b_qlat if qk else b_klat
                bias_r = b_qlat_r if qk else b_klat_r
                bias_res = [p.R("bql" if qk else "bkl"),
                            p.R("bqlr" if qk else "bklr")]
                p.op("vector", lambda eng, bk=bk: eng.stream_shuffle(
                    qrot[:], bank_ap(bk, 0, TTA), PAIR_MASK),
                    reads=[BK[bk]], writes=[p.R("qrot")])
                p.op("vector", lambda eng, bk=bk, hg=hg, bias=bias, g0=g0:
                     eng.scalar_tensor_tensor(
                         t1_sb[:], bank_ap(bk, 0, TTA), bias[:, hg:hg + 1],
                         cos_bf[:, g0:g0 + TTA], ALU.add, ALU.mult),
                     reads=[BK[bk], bias_res[0], p.R("cos")], writes=[p.R("t1")])
                p.op("vector", lambda eng, hg=hg, bias_r=bias_r, g0=g0:
                     eng.scalar_tensor_tensor(
                         t2_sb[:], qrot[:], bias_r[:, hg:hg + 1],
                         sin_bf[:, g0:g0 + TTA], ALU.add, ALU.mult),
                     reads=[p.R("qrot"), bias_res[1], p.R("sin")],
                     writes=[p.R("t2")])
                if qk:
                    qpar = m % 2
                    p.op("vector", lambda eng, qpar=qpar: eng.tensor_tensor(
                        qp_sb[:, TTA * qpar:TTA * qpar + TTA],
                        t1_sb[:], t2_sb[:], ALU.add),
                        reads=[p.R("t1"), p.R("t2")], writes=[p.R("qp", qpar)])

                    def q_follow(hg=hg, gt0=gt0, ta=ta, nb=nb, qpar=qpar):
                        fns = []
                        for i in range(4):
                            fns.append(lambda eng, i=i, nb=nb, qpar=qpar:
                                       eng.matmul(
                                bank_ap(nb, 0, TTA)[32 * i:32 * i + 32, :],
                                wqk4_bf[32 * i:32 * i + 32, :],
                                qp_sb[32 * i:32 * i + 32,
                                      TTA * qpar:TTA * qpar + TTA],
                                start=True, stop=True,
                                tile_position=(32 * i, 32 * i)))
                        p.group("tensor", fns,
                                reads=[p.R("qp", qpar), p.R("wqk4bf")],
                                writes=[BK[nb]])
                        qdst = p.R("q", hg, ta)
                        p.op("vector", lambda eng, hg=hg, gt0=gt0, nb=nb:
                             eng.tensor_scalar(
                                 q_store[:, seq * hg + gt0:seq * hg + gt0 + TTA],
                                 bank_ap(nb, 0, TTA), b_wqks[:, 0:1], ISQL,
                                 ALU.add, ALU.mult),
                             reads=[BK[nb], p.R("bwqks")], writes=[qdst])
                        par2 = hg % 2
                        p.op("scalar", lambda eng, nb=nb, par2=par2:
                             eng.activation(
                                 sq2_sb[:, TTA * par2:TTA * par2 + TTA],
                                 bank_ap(nb, 0, TTA), AF.Square, scale=ISQL),
                             reads=[BK[nb]], writes=[p.R("sq2", par2)])

                    def q_norm(hg=hg, ta=ta):
                        par2 = hg % 2
                        p.group("tensor", [lambda eng, par2=par2: eng.matmul(
                            bank_ap(7, 0, TTA)[0:4, :], onesblk_sb[:],
                            sq2_sb[:, TTA * par2:TTA * par2 + TTA],
                            start=True, stop=True)],
                            reads=[p.R("sq2", par2), p.R("onesblk")],
                            writes=[BK[7]])
                        p.op("vector", lambda eng, hg=hg, ta=ta:
                             eng.reduce_max(
                                 qn_acc[:, 4 * ta + hg:4 * ta + hg + 1],
                                 bank_ap(7, 0, TTA)[0:4, :], axis=AX.X),
                             reads=[BK[7]], writes=[p.R("qn", 4 * ta + hg)])
                    defer(m + 2, q_follow)
                    defer(m + 3, q_norm)
                else:
                    kdst = p.R("k", hg, ta)
                    p.op("vector", lambda eng, hg=hg, gt0=gt0: eng.tensor_tensor(
                        k_store[:, seq * hg + gt0:seq * hg + gt0 + TTA],
                        t1_sb[:], t2_sb[:], ALU.add),
                        reads=[p.R("t1"), p.R("t2")], writes=[kdst])

                    def k_follow(hg=hg, gt0=gt0, ta=ta, kdst=kdst):
                        par2 = hg % 2
                        p.op("vector", lambda eng, hg=hg, gt0=gt0, par2=par2:
                             eng.tensor_tensor(
                                 sq2_sb[:, TTA * par2:TTA * par2 + TTA],
                                 k_store[:, seq * hg + gt0:seq * hg + gt0 + TTA],
                                 k_store[:, seq * hg + gt0:seq * hg + gt0 + TTA],
                                 ALU.mult),
                             reads=[kdst], writes=[p.R("sq2", par2)])

                    def k_norm(hg=hg, ta=ta):
                        par2 = hg % 2
                        p.group("tensor", [lambda eng, par2=par2: eng.matmul(
                            bank_ap(7, 0, TTA)[0:4, :], onesblk_sb[:],
                            sq2_sb[:, TTA * par2:TTA * par2 + TTA],
                            start=True, stop=True)],
                            reads=[p.R("sq2", par2), p.R("onesblk")],
                            writes=[BK[7]])
                        p.op("vector", lambda eng, hg=hg, ta=ta:
                             eng.reduce_max(
                                 kn_acc[:, 4 * ta + hg:4 * ta + hg + 1],
                                 bank_ap(7, 0, TTA)[0:4, :], axis=AX.X),
                             reads=[BK[7]], writes=[p.R("kn", 4 * ta + hg)])
                    defer(m + 2, k_follow)
                    defer(m + 3, k_norm)
            else:
                hp = m - 8
                vpar = m % 2
                p.op("vector", lambda eng, bk=bk, vpar=vpar: eng.tensor_scalar(
                    v_sb[:, TTA * vpar:TTA * vpar + TTA],
                    bank_ap(bk, 0, TTA), b_vo2[:, 0:1], None, ALU.add),
                    reads=[BK[bk], p.R("bvo")], writes=[p.R("vsb", vpar)])
                nsub = TTA // 128
                cg0 = (HALF * hb + g0) // 128
                vdst = [p.R("v", cg0 + jj) for jj in range(nsub)]

                def v_follow(hp=hp, nsub=nsub, cg0=cg0, vdst=vdst, vpar=vpar):
                    for hh2 in range(2):
                        base = 5120 - 1024 * hh2
                        fns = []
                        for jj in range(nsub):
                            fns.append(lambda eng, jj=jj, hh2=hh2, vpar=vpar,
                                       base=base: eng.transpose(
                                PSB[:, base + 64 * jj:base + 64 * jj + 64],
                                v_sb[64 * hh2:64 * hh2 + 64,
                                     TTA * vpar + 128 * jj:TTA * vpar + 128 * jj + 128],
                                id2_sb[64 * hh2:64 * hh2 + 64, :]))
                        p.group("tensor", fns,
                                reads=[p.R("vsb", vpar), p.R("id2")],
                                writes=[BK[5 - hh2]])
                        p.op("vector", lambda eng, cg0=cg0, hp=hp, hh2=hh2,
                             nsub=nsub, vst4=vst4, base=base: eng.tensor_copy(
                                 vst4[:, cg0:cg0 + nsub, 2 * hp + hh2, 0:64],
                                 PSB.ap().rearrange("p (j e) -> p j e",
                                                    j=128, e=64)[:, base // 64:base // 64 + nsub, :]),
                             reads=[BK[5 - hh2]], writes=vdst)
                defer(m + 2, v_follow)
        flush()

    def emit_bexp(b, hb):
        ncol = 4 * (2 * hb + 2) if TTA == 512 else 4 * NT * (hb + 1)
        qres = [p.R("qn", i) for i in range(ncol)]
        kres = [p.R("kn", i) for i in range(ncol)]
        p.op("vector", lambda eng, ncol=ncol: eng.reduce_max(
            qmx[:], qn_acc[:, 0:ncol], axis=AX.X), reads=qres, writes=[p.R("qmx")])
        p.op("vector", lambda eng, ncol=ncol: eng.reduce_max(
            kmx[:], kn_acc[:, 0:ncol], axis=AX.X), reads=kres, writes=[p.R("kmx")])
        p.op("vector", lambda eng: eng.tensor_tensor(
            prod4[:], qmx[:], kmx[:], ALU.mult),
            reads=[p.R("qmx"), p.R("kmx")], writes=[p.R("prod4")])
        p.op("sync", lambda eng: eng.dma_start(out=g14[:], in_=prod4[:]),
             reads=[p.R("prod4")], writes=[p.R("g14")], dma="LDI")
        p.op("vector", lambda eng: eng.reduce_max(mmax[:], g14[:], axis=AX.X),
             reads=[p.R("g14")], writes=[p.R("mmax")])
        p.op("scalar", lambda eng: eng.activation(msqrt[:], mmax[:], AF.Sqrt),
             reads=[p.R("mmax")], writes=[p.R("msqrt")])
        p.op("vector", lambda eng: eng.tensor_single_scalar(
            mneg_bf[:], msqrt[:], -1.0, ALU.mult),
            reads=[p.R("msqrt")], writes=[p.R("mneg")])
        p.group("tensor", [lambda eng: eng.matmul(
            bank_ap(0, 0, 1), ones1x_sb[:], mneg_bf[:], start=True, stop=True)],
            reads=[p.R("mneg"), p.R("ones1x")], writes=[BK[0]])
        p.op("scalar", lambda eng: eng.activation(
            b_exp[:], bank_ap(0, 0, 1), AF.Copy),
            reads=[BK[0]], writes=[p.R("bexp")])

    def emit_attn_pair(b, hb, h0, defer_in=None):
        nchunks = (hb + 1) * CH
        regions = [(0, min(512, QW))] + ([(512, QW)] if QW > 512 else [])
        lastc = {}
        for (lo, hi) in regions:
            lastc[(lo, hi)] = max(c for c in range(nchunks)
                                  if max(0, 128 * c - HALF * hb) < hi)

        def s_chunk(h, c):
            hg, gq = h % 4, h // 4
            par = h % 2
            qoff = max(0, 128 * c - HALF * hb)
            ta_k = (128 * c) // TTA
            kread = p.R("k", gq, ta_k)
            qread = [p.R("q", gq, (HALF * hb) // TTA + t) for t in range(NT)]
            for (lo, hi) in regions:
                a, bnd = max(lo, qoff), hi
                if a >= bnd:
                    continue
                bkidx = 2 * par + lo // 512
                p.group("tensor", [lambda eng, a=a, bnd=bnd, par=par, c=c,
                                   hg=hg, gq=gq: eng.matmul(
                    PS[:, 1024 * par + a:1024 * par + bnd],
                    k_store[32 * hg:32 * hg + 32,
                            seq * gq + 128 * c:seq * gq + 128 * c + 128],
                    q_store[32 * hg:32 * hg + 32,
                            seq * gq + HALF * hb + a:seq * gq + HALF * hb + bnd],
                    start=True, stop=True, tile_position=(32 * hg, 0))],
                    reads=[kread] + qread, writes=[BK[bkidx]])
            slot = 2 * par + c % 2
            rbanks = [BK[2 * par + i] for i in ((0, 1) if (qoff < 512 and QW > 512)
                                               else ((0,) if qoff < 512 else (1,)))]
            p.op("scalar", lambda eng, slot=slot, par=par, qoff=qoff:
                 eng.activation(
                     PT[:, QW * slot + qoff:QW * slot + QW],
                     PS[:, 1024 * par + qoff:1024 * par + QW], AF.Exp,
                     bias=b_exp[:, 0:1]),
                 reads=rbanks + [p.R("bexp")], writes=[p.R("pt", slot)])
            if 128 * c >= HALF * hb:
                p.op("vector", lambda eng, slot=slot, qoff=qoff: eng.tensor_tensor(
                    PT[:, QW * slot + qoff:QW * slot + qoff + 128],
                    PT[:, QW * slot + qoff:QW * slot + qoff + 128],
                    causal_sb[:], ALU.mult),
                    reads=[p.R("causal"), p.R("pt", slot)], writes=[p.R("pt", slot)])

        def pv_chunk(h, c):
            par = h % 2
            slot = 2 * par + c % 2
            qoff = max(0, 128 * c - HALF * hb)
            yb = 2048 + 1024 * par
            vread = [p.R("v", c), p.R("vones")]
            for (lo, hi) in regions:
                a, bnd = max(lo, qoff), hi
                if a >= bnd:
                    continue
                startv = (c == 0)
                stopv = (c == lastc[(lo, hi)])
                wb = [BK[4 + 2 * par]] if a < 512 else [BK[5 + 2 * par]]
                p.group("tensor", [lambda eng, a=a, bnd=bnd, slot=slot,
                                   c=c, h=h, startv=startv, stopv=stopv,
                                   yb=yb: eng.matmul(
                    PS[0:65, yb + a:yb + bnd],
                    v_store[:, 1040 * c + 65 * h:1040 * c + 65 * h + 65],
                    PT[:, QW * slot + a:QW * slot + bnd],
                    start=startv, stop=stopv)],
                    reads=vread + [p.R("pt", slot)],
                    writes=wb)

        for c in range(nchunks):
            s_chunk(h0, c)
            s_chunk(h0 + 1, c)
            if c == min(1, nchunks - 1) and defer_in is not None:
                defer_in[0]()
                if c >= 1:
                    pv_chunk(h0, c - 1)
                defer_in[1]()
                if c >= 1:
                    pv_chunk(h0 + 1, c - 1)
                defer_in = None
            elif c >= 1:
                pv_chunk(h0, c - 1)
                pv_chunk(h0 + 1, c - 1)
        pv_chunk(h0, nchunks - 1)
        pv_chunk(h0 + 1, nchunks - 1)

        yball = [BK[4], BK[6]] + ([BK[5], BK[7]] if QW > 512 else [])
        p.op("scalar", lambda eng: eng.activation(
            r_sb.ap().rearrange("p (s w) -> p s w", w=QW)[0:1, 0:2, :],
            PS.ap().rearrange("p (s w) -> p s w", w=1024)[64:65, 2:4, 0:QW],
            AF.Ln),
            reads=yball, writes=[p.R("r")])
        p.op("scalar", lambda eng: eng.activation(
            r_bf[:, 0:2 * QW], r_sb[:, 0:2 * QW], AF.Exp, scale=-1.0),
            reads=[p.R("r")], writes=[p.R("rbf", 0), p.R("rbf", 1)])

        def mk_epilogue(h):
            par = h % 2
            yb = 2048 + 1024 * par
            ybanks = [BK[4 + 2 * par]] + ([BK[5 + 2 * par]] if QW > 512 else [])

            def epilogue(h=h, yb=yb, ybanks=ybanks, par=par):
                p.group("tensor", [lambda eng, yb=yb, par=par: eng.matmul(
                    PS[64:128, yb:yb + min(512, QW)], ones1x_sb[0:1, 0:64],
                    r_bf[:, QW * par:QW * par + min(512, QW)],
                    start=True, stop=True)] +
                    ([lambda eng, yb=yb, par=par: eng.matmul(
                        PS[64:128, yb + 512:yb + QW], ones1x_sb[0:1, 0:64],
                        r_bf[:, QW * par + 512:QW * par + QW],
                        start=True, stop=True)] if QW > 512 else []),
                    reads=[p.R("rbf", par), p.R("ones1x")],
                    writes=ybanks)
                p.op("vector", lambda eng, yb=yb, par=par: eng.tensor_copy(
                    rb_sb[:, QW * par:QW * par + QW], PS[64:128, yb:yb + QW]),
                    reads=ybanks, writes=[p.R("rb", par)])
                hp, sub = h // 2, h % 2
                p.op("vector", lambda eng, hp=hp, sub=sub, yb=yb, par=par:
                     eng.tensor_tensor(
                         y_sb[64 * sub:64 * sub + 64, QW * hp:QW * hp + QW],
                         PS[0:64, yb:yb + QW], rb_sb[:, QW * par:QW * par + QW],
                         ALU.mult),
                     reads=ybanks + [p.R("rb", par)], writes=[p.R("ysb", hp, sub)])
            return epilogue

        ep0 = mk_epilogue(h0)
        ep1 = mk_epilogue(h0 + 1)

        return (ep0, ep1)

    ceff_done = [False]

    def emit_cproj(b, hb):
        gctr = [0]
        segs = [(0, min(512, QW))] + ([(512, QW)] if QW > 512 else [])
        for si, (lo, hi) in enumerate(segs):
            for o in range(NCH):
                j, sres = stream_slot(cp_strm[o, :, :], p.R("cp_strm", o))
                if not ceff_done[0]:
                    bk = 6 + gctr[0] % 2
                    gctr[0] += 1
                    fns = []
                    for dd in range(NCH):
                        fns.append(lambda eng, j=j, dd=dd, bk=bk: eng.matmul(
                            bank_ap(bk, 0, 1), slot_ap(j, dd),
                            tl2o_bf[:, dd:dd + 1],
                            start=(dd == 0), stop=(dd == NCH - 1)))
                    p.group("tensor", fns, reads=[sres, p.R("tl2obf")],
                            writes=[BK[bk]])
                    p.op("vector", lambda eng, o=o, bk=bk: eng.tensor_tensor(
                        c_eff[:, o:o + 1], bank_ap(bk, 0, 1), cb_sb[:, o:o + 1],
                        ALU.add),
                        reads=[BK[bk], p.R("cb")], writes=[p.R("ceff", o)])
                bk = 6 + gctr[0] % 2
                gctr[0] += 1
                fns = []
                for dd in range(NCH):
                    fns.append(lambda eng, j=j, dd=dd, bk=bk, lo=lo, hi=hi:
                               eng.matmul(
                        bank_ap(bk, 0, hi - lo), slot_ap(j, dd),
                        y_sb[:, QW * dd + lo:QW * dd + hi],
                        start=(dd == 0), stop=(dd == NCH - 1)))
                p.group("tensor", fns, reads=[sres], writes=[BK[bk]],
                        fn_reads=[[p.R("ysb", dd, 0), p.R("ysb", dd, 1)]
                                  for dd in range(NCH)])
                tt = lo // TTA
                xr = p.R("x", o, tt)
                p.op("vector", lambda eng, o=o, lo=lo, hi=hi, bk=bk:
                     eng.scalar_tensor_tensor(
                         x_ap(o, (lo, hi)), bank_ap(bk, 0, hi - lo),
                         c_eff[:, o:o + 1], x_ap(o, (lo, hi)),
                         ALU.add, ALU.add),
                     reads=[BK[bk], xr, p.R("ceff", o)], writes=[xr])
                if si == len(segs) - 1 and o >= 3:
                    ln_stats_chunk(0, o - 3, sqb=2)
            if si == len(segs) - 1:
                ln_stats_chunk(0, 5, sqb=2)
                ln_stats_chunk(0, 6, sqb=2)
                ln_stats_chunk(0, 7, sqb=2)
        ceff_done[0] = True

    def emit_store(b, hb):
        for o in range(NCH):
            xres = [p.R("x", o, tt) for tt in range(NT)]
            p.op("scalar", lambda eng, o=o, b=b, hb=hb: eng.dma_start(
                out=d_out[b, 128 * o:128 * o + 128, HALF * hb:HALF * hb + HALF],
                in_=x_sb[:, HALF * o:HALF * o + HALF]),
                reads=xres, writes=[p.R("outdram", b, hb, o)], dma="ST")

    vst4 = v_store.ap().rearrange("p (c h e) -> p c h e", c=CHB, h=16, e=65)
    stop = DEBUG_STOP
    prefetched = [False]
    for b in range(bpc):
        if stop == "init":
            break
        p.op("vector", lambda eng: eng.memset(vst4[:, :, :, 64:65], 1.0),
             reads=[], writes=[p.R("vones")])
        for hb in range(2):
            for nm, dsrc, ddst in (("cos", d_cos4, cos_bf),
                                   ("sin", d_sin4, sin_bf)):
                ld(ddst[:], dsrc[:, HALF * hb:HALF * hb + HALF], p.R(nm))
            pf = prefetched[0]
            prefetched[0] = False
            if not pf and not (b == 0 and hb == 0):
                for o in range(NCH):
                    p.op("sync", lambda eng, b=b, hb=hb, o=o: eng.dma_start(
                        out=x_sb[:, HALF * o:HALF * o + HALF],
                        in_=d_x[b, 128 * o:128 * o + 128,
                                HALF * hb:HALF * hb + HALF]),
                        writes=[p.R("x", o, tt) for tt in range(NT)],
                        dma="LDI")
            if stop != "xload":
                if pf:
                    ln_finish(b, hb, 0, sqb=3)
                else:
                    for c in range(NCH):
                        ln_stats_chunk(0, c)
                    ln_finish(b, hb, 0)
                for tt in range(NT):
                    if stop == "ln1":
                        break
                    emit_ffn(b, hb, tt, ln_next=tt)
                    if stop == "ffn1":
                        break
                    ln_finish(b, hb, tt)
                    emit_cattn(b, hb, tt,
                               ln_next=(tt + 1 if tt + 1 < NT else None))
                    if tt + 1 < NT:
                        ln_finish(b, hb, tt + 1)
                if stop in ("ln1", "ffn1", "cattn"):
                    emit_store(b, hb)
                    break
                emit_bexp(b, hb)
                if stop == "bexp":
                    emit_store(b, hb)
                    break
                epi = None
                for hp0 in range(NHEAD // 2):
                    epi = emit_attn_pair(b, hb, 2 * hp0, defer_in=epi)
                if epi is not None:
                    epi[0]()
                    epi[1]()
                    epi = None
                if stop == "heads":
                    emit_store(b, hb)
                    break
                emit_cproj(b, hb)
                if stop == "cproj":
                    emit_store(b, hb)
                    break
                ln_finish(b, hb, 0, sqb=2)
                nxt = (b, 1) if hb == 0 else ((b + 1, 0) if b + 1 < bpc else None)
                emit_ffn(b, hb, 0, ln_next=(1 if NT > 1 else None), lnbank=2,
                         store=True, nxt=nxt)
                if NT > 1:
                    ln_finish(b, hb, 1, sqb=2)
                    emit_ffn(b, hb, 1, store=True, nxt=nxt)
                prefetched[0] = nxt is not None
            else:
                emit_store(b, hb)
        else:
            continue
        break

    for s in ("ST0", "ST1", "ST2", "ST3", "ST4", "ST5", "ST6", "ST7"):
        p.raw_wait("scalar", s, p.cnt[s])
        p.raw_wait("sync", s, p.cnt[s])

    def replay(stream):
        def body(eng):
            for e in stream:
                if e[0] == "w":
                    eng.wait_ge(sems[e[1]], e[2])
                else:
                    ins = e[1](eng)
                    if e[2] is not None:
                        ins.then_inc(sems[e[2]], e[3])
        return body

    with nc.Block() as blk:
        blk.sync(replay(p.streams["sync"]))
        blk.vector(replay(p.streams["vector"]))
        blk.tensor(replay(p.streams["tensor"]))
        blk.scalar(replay(p.streams["scalar"]))
        blk.gpsimd(replay(p.streams["gpsimd"]))

    st.__exit__(None, None, None)
    return nc


def _prep_shared(inputs, seq):
    W = np.ascontiguousarray(inputs["L0_attn_c_attn_w"].astype(np.float32))
    consts = _host_consts(seq)
    sh = {
        "w_attn": W,
        "b_attn48": np.ascontiguousarray(
            inputs["L0_attn_c_attn_b"].astype(np.float32).reshape(48, 64).T),
        "q2l_wT": np.ascontiguousarray(inputs["L0_attn_q2l_w"].astype(np.float32).T),
        "q2l_b": inputs["L0_attn_q2l_b"].astype(np.float32).reshape(32, 1),
        "kv2l_w": np.ascontiguousarray(inputs["L0_attn_kv2l_w"].astype(np.float32)),
        "kv2l_wT": np.ascontiguousarray(inputs["L0_attn_kv2l_w"].astype(np.float32).T),
        "kv2l_b": inputs["L0_attn_kv2l_b"].astype(np.float32).reshape(32, 1),
        "l2o_wT": np.ascontiguousarray(inputs["L0_attn_l2o_w"].astype(np.float32).T),
        "tl2o8": np.ascontiguousarray(
            np.tile(inputs["L0_attn_l2o_b"].astype(np.float32), 16).reshape(8, 128).T),
        "wqk_wT4": np.ascontiguousarray(
            np.tile(inputs["L0_attn_wqk_w"].astype(np.float32).T, (4, 1))),
        "wqk_b4": np.ascontiguousarray(
            np.tile(inputs["L0_attn_wqk_b"].astype(np.float32), 4).reshape(128, 1)),
        "cprojT": np.ascontiguousarray(inputs["L0_attn_cproj_w"].astype(np.float32).T),
        "cproj_b8": np.ascontiguousarray(
            inputs["L0_attn_cproj_b"].astype(np.float32).reshape(8, 128).T),
        "ffnT": np.ascontiguousarray(inputs["L0_ffn_weight"].astype(np.float32).T),
        "disha2": np.ascontiguousarray(
            np.concatenate([inputs["L0_ffn_disha"].astype(np.float32)] * 2, axis=0)),
        "cos4": consts["cos4"], "sin4": consts["sin4"],
        "causal": consts["causal"], "id2": consts["id2"],
        "id128": consts["id128"], "maskbig": consts["maskbig"],
        "ones128": consts["ones128"], "ones1x": consts["ones1x"],
        "onesblk": consts["onesblk"],
    }
    return sh


def run(inputs, bpc, seq, n_cores):
    x = np.asarray(inputs["L0_attn_x"], dtype=np.float32)
    B = x.shape[0]
    assert B == bpc * n_cores
    sh = _prep_shared(inputs, seq)
    nc = build_program(bpc, seq)
    in_maps = []
    for c in range(n_cores):
        m = dict(sh)
        m["x"] = np.ascontiguousarray(
            x[bpc * c:bpc * (c + 1)].transpose(0, 2, 1))
        in_maps.append(m)
    res = run_bass_kernel_spmd(nc, in_maps, core_ids=list(range(n_cores)))
    outs = [np.asarray(r["out"]) for r in res.results]
    out = np.concatenate(outs, axis=0)
    return np.ascontiguousarray(out.transpose(0, 2, 1))


def kernel(**inputs):
    return run(inputs, bpc=4, seq=2048, n_cores=8)


# version 40
# speedup vs baseline: 1.0227x; 1.0227x over previous
import math
import numpy as np
import ml_dtypes
import contextlib
import concourse.bass as bass
import concourse.mybir as mybir
from concourse.bass_utils import run_bass_kernel_spmd

F32 = mybir.dt.float32
BF16 = mybir.dt.bfloat16
AF = mybir.ActivationFunctionType
ALU = mybir.AluOpType
AX = mybir.AxisListType
BF = ml_dtypes.bfloat16

N_CORES = 8
DEBUG_STOP = None
D = 1024
NCH = 8
NHEAD = 16
HD = 64
LAT = 32
ISQL = 1.0 / math.sqrt(LAT)

PAIR_MASK = []
for _g in range(16):
    PAIR_MASK += [2 * _g + 1, 2 * _g]


class Res:
    __slots__ = ("w", "r", "bank")

    def __init__(self, bank=False):
        self.w = None
        self.r = {}
        self.bank = bank


class Prog:
    ENGINES = ("sync", "vector", "tensor", "scalar", "gpsimd")
    ESEM = dict(sync="LDI0", vector="DV", tensor="PE", scalar="AC", gpsimd="PL")
    CONS_DMA = ("LDI0", "LDI1", "LDI2", "LDI3", "LDI4", "LDI5",
                "ST0", "ST1", "ST2", "ST3", "ST4", "ST5", "ST6", "ST7",
                "PF0", "PF1", "PF2", "PF3", "PF4", "PF5", "PF6", "PF7")
    SEM_NAMES = ("LDI0", "LDI1", "LDI2", "LDI3", "LDI4", "LDI5",
                 "ST0", "ST1", "ST2", "ST3", "ST4", "ST5", "ST6", "ST7",
                 "PF0", "PF1", "PF2", "PF3", "PF4", "PF5", "PF6", "PF7",
                 "PE", "DV", "AC", "PL",
                 "SL0", "SL1", "SL2", "SL3", "SL4", "SL5", "SL6")

    def __init__(self):
        self.streams = {e: [] for e in self.ENGINES}
        self.cnt = {s: 0 for s in self.SEM_NAMES}
        self.lastw = {}
        self.res = {}
        self.ldi_ctr = 0
        self.st_ctr = 0
        self.pf_ctr = 0

    def R(self, *key):
        kind = key[0]
        return self.res.setdefault(key, Res(bank=(kind == "bank")))

    def _wait(self, eng, tok):
        if tok is None:
            return
        sem, val = tok

        k = (eng, sem)
        if self.lastw.get(k, 0) >= val:
            return
        self.lastw[k] = val
        self.streams[eng].append(("w", sem, val))

    def _pre(self, eng, reads, writes):
        own = self.ESEM[eng]
        for r in reads:
            self._wait(eng, r.w)
            if r.bank:
                for s, v in r.r.items():
                    if s != own:
                        self._wait(eng, (s, v))
        for w in writes:
            self._wait(eng, w.w)
            for s, v in w.r.items():
                self._wait(eng, (s, v))

    def _post(self, tok, reads, writes):
        for r in reads:
            r.r[tok[0]] = max(r.r.get(tok[0], 0), tok[1])
        for w in writes:
            w.w = tok
            w.r = {}

    def op(self, eng, fn, reads=(), writes=(), dma=None):
        self._pre(eng, reads, writes)
        if dma:
            if dma == "LDI":
                dma = "LDI%d" % (self.ldi_ctr % 6)
                self.ldi_ctr += 1
                if self.cnt[dma]:
                    self._wait(eng, (dma, self.cnt[dma]))
            elif dma == "PF":
                dma = "PF%d" % (self.pf_ctr % 8)
                self.pf_ctr += 1
                if self.cnt[dma]:
                    self._wait(eng, (dma, self.cnt[dma]))
            elif dma == "ST":
                dma = "ST%d" % (self.st_ctr % 8)
                self.st_ctr += 1
                if self.cnt[dma]:
                    self._wait(eng, (dma, self.cnt[dma]))
            sem = dma
            inc = 16
        else:
            sem = self.ESEM[eng]
            inc = 1
        self.cnt[sem] += inc
        tok = (sem, self.cnt[sem])
        self.streams[eng].append(("i", fn, sem, inc))
        self._post(tok, reads, writes)
        return tok

    def group(self, eng, fns, reads=(), writes=(), fn_reads=None):
        self._pre(eng, reads, writes)
        sem = self.ESEM[eng]
        for i, f in enumerate(fns):
            if fn_reads is not None:
                for r in fn_reads[i]:
                    self._wait(eng, r.w)
            last = (i == len(fns) - 1)
            if last:
                self.cnt[sem] += 1
                self.streams[eng].append(("i", f, sem, 1))
            else:
                self.streams[eng].append(("i", f, None, 0))
        tok = (sem, self.cnt[sem])
        all_reads = list(reads)
        if fn_reads is not None:
            for fr in fn_reads:
                all_reads += list(fr)
        self._post(tok, all_reads, writes)
        return tok

    def raw_wait(self, eng, sem, val):
        self.streams[eng].append(("w", sem, val))


def _host_consts(seq):
    a = np.arange(0, LAT, 2, dtype=np.float32) / np.float32(LAT)
    inv = (np.float32(1.0) / (np.float32(10000.0) ** a)).astype(np.float32)
    t = np.arange(seq, dtype=np.float32)
    freqs = np.outer(t, inv).astype(np.float32)
    cos = np.repeat(np.cos(freqs), 2, axis=1)[:, :LAT]
    sin = np.repeat(np.sin(freqs), 2, axis=1)[:, :LAT]
    sgn = np.where(np.arange(LAT) % 2 == 0, -1.0, 1.0).astype(np.float32)
    sin_s = sin * sgn[None, :]
    cos4 = np.ascontiguousarray(np.tile(cos.T, (4, 1)))
    sin4 = np.ascontiguousarray(np.tile(sin_s.T, (4, 1)))
    causal = (np.arange(128)[:, None] <= np.arange(128)[None, :]).astype(np.float32)
    maskbig = np.where(np.arange(128)[:, None] > np.arange(128)[None, :],
                       np.float32(-1e30), np.float32(0.0))
    id2 = np.vstack([np.eye(64, dtype=np.float32)] * 2)
    onesblk = np.zeros((128, 4), np.float32)
    for g in range(4):
        onesblk[32 * g:32 * g + 32, g] = 1.0
    return dict(
        cos4=cos4.astype(BF), sin4=sin4.astype(BF),
        causal=causal.astype(BF), id2=id2.astype(BF),
        id128=np.eye(128, dtype=np.float32).astype(BF),
        maskbig=maskbig.astype(BF),
        ones128=np.ones((128, 1), np.float32).astype(BF),
        ones1x=np.ones((1, 128), np.float32).astype(BF),
        onesblk=onesblk.astype(BF),
    )


def _finish(nc, p, sems, st):
    def replay(stream):
        def body(eng):
            for e in stream:
                if e[0] == "w":
                    eng.wait_ge(sems[e[1]], e[2])
                else:
                    ins = e[1](eng)
                    if e[2] is not None:
                        ins.then_inc(sems[e[2]], e[3])
        return body
    with nc.Block() as blk:
        blk.sync(replay(p.streams["sync"]))
        blk.vector(replay(p.streams["vector"]))
        blk.tensor(replay(p.streams["tensor"]))
        blk.scalar(replay(p.streams["scalar"]))
        blk.gpsimd(replay(p.streams["gpsimd"]))
    st.__exit__(None, None, None)
    return nc


def build_program(bpc, seq):
    HALF = seq // 2
    TTA = min(512, HALF)
    NT = HALF // TTA
    CH = HALF // 128
    CHB = seq // 128
    NTB = seq // TTA
    QW = HALF

    nc = bass.Bass("TRN2", target_bir_lowering=False, debug=False)

    def dram(name, shape, dt=F32, kind="ExternalInput"):
        return nc.dram_tensor(name, shape, dt, kind=kind).ap()

    d_x = dram("x", [bpc, D, seq])
    d_out = dram("out", [bpc, D, seq], kind="ExternalOutput")
    d_wattn = dram("w_attn", [3 * D, D])
    d_ba48 = dram("b_attn48", [64, 48])
    d_q2lT = dram("q2l_wT", [64, 32])
    d_q2lb = dram("q2l_b", [32, 1])
    d_kv2lw = dram("kv2l_w", [32, 64])
    d_kv2lT = dram("kv2l_wT", [64, 32])
    d_kv2lb = dram("kv2l_b", [32, 1])
    d_l2oT = dram("l2o_wT", [32, 64])
    d_tl2o8 = dram("tl2o8", [128, 8])
    d_wqkT4 = dram("wqk_wT4", [128, 32])
    d_wqkb4 = dram("wqk_b4", [128, 1])
    d_cpT = dram("cprojT", [D, D])
    d_cpb8 = dram("cproj_b8", [128, 8])
    d_ffnT = dram("ffnT", [D, D])
    d_disha2 = dram("disha2", [128, D])
    d_cos4 = dram("cos4", [128, seq], BF16)
    d_sin4 = dram("sin4", [128, seq], BF16)
    d_causal = dram("causal", [128, 128], BF16)
    d_id128 = dram("id128", [128, 128], BF16)
    d_maskbig = dram("maskbig", [128, 128], BF16)
    d_id2 = dram("id2", [128, 64], BF16)
    d_ones128 = dram("ones128", [128, 1], BF16)
    d_ones1x = dram("ones1x", [1, 128], BF16)
    d_onesblk = dram("onesblk", [128, 4], BF16)

    wa_strm = dram("wa_strm", [16, 128, 8 * 128], BF16, kind="Internal")
    me_cmaj = dram("me_cmaj", [8, 128, 8 * 128], BF16, kind="Internal")
    me_strm = dram("me_strm", [8, 128, 8 * 128], BF16, kind="Internal")
    cp_cmaj = dram("cp_cmaj", [8, 128, 8 * 128], BF16, kind="Internal")
    cp_strm = dram("cp_strm", [8, 128, 8 * 128], BF16, kind="Internal")

    st = contextlib.ExitStack()
    st.__enter__()

    def sb(name, shape, dt):
        return st.enter_context(nc.sbuf_tensor(name, shape, dt))

    x_sb = sb("x_sb", [128, 8 * HALF], F32)
    xb_sb = sb("xb_sb", [128, 8 * TTA], BF16)
    sq_sb = sb("sq_sb", [128, 2 * TTA], BF16)
    ln_sb = sb("ln_sb", [128, 8 * TTA], BF16)
    mb_sb = sb("mb_sb", [128, TTA], BF16)
    ab_sb = sb("ab_sb", [128, TTA], BF16)
    m_f = sb("m_f", [1, TTA], F32)
    m_bf = sb("m_bf", [1, TTA], BF16)
    msq = sb("msq", [1, TTA], F32)
    var_f = sb("var_f", [1, TTA], F32)
    a_f = sb("a_f", [1, TTA], F32)
    a_bf = sb("a_bf", [1, TTA], BF16)
    qrot = sb("qrot", [128, TTA], F32)
    t1_sb = sb("t1_sb", [128, TTA], BF16)
    t2_sb = sb("t2_sb", [128, TTA], BF16)
    qp_sb = sb("qp_sb", [128, 2 * TTA], BF16)
    v_sb = sb("v_sb", [128, 2 * TTA], BF16)
    sq2_sb = sb("sq2_sb", [128, 2 * TTA], BF16)
    q_store = sb("q_store", [128, 4 * seq], BF16)
    k_store = sb("k_store", [128, 4 * seq], BF16)
    v_store = sb("v_store", [128, CHB * 1040], BF16)
    qn_acc = sb("qn_acc", [4, 4 * NTB], F32)
    kn_acc = sb("kn_acc", [4, 4 * NTB], F32)
    qmx = sb("qmx", [4, 1], F32)
    kmx = sb("kmx", [4, 1], F32)
    prod4 = sb("prod4", [4, 1], F32)
    g14 = sb("g14", [1, 4], F32)
    mmax = sb("mmax", [1, 1], F32)
    msqrt = sb("msqrt", [1, 1], F32)
    mneg_bf = sb("mneg_bf", [1, 1], BF16)
    b_exp = sb("b_exp", [128, 1], F32)
    PT = sb("PT", [128, 4 * QW], BF16)
    y_sb = sb("y_sb", [128, 8 * QW], BF16)
    r_sb = sb("r_sb", [1, 2 * QW], F32)
    r_bf = sb("r_bf", [1, 2 * QW], BF16)
    rb_sb = sb("rb_sb", [64, 2 * QW], BF16)
    NSLOT = 7
    wslot = sb("wslot", [128, NSLOT * 1024], BF16)
    stg = sb("stg", [128, 1024], F32)
    build_sb = sb("build_sb", [128, 1024], BF16)
    slab1 = sb("slab1", [64, 1024], BF16)
    ba_sb = sb("ba_sb", [64, 48], F32)
    ba_bf = sb("ba_bf", [64, 48], BF16)
    tq2lT = sb("tq2lT", [64, 32], F32)
    tkv2lT = sb("tkv2lT", [64, 32], F32)
    q2lb_sb = sb("q2lb_sb", [32, 1], F32)
    kv2lb_sb = sb("kv2lb_sb", [32, 1], F32)
    kvb_bf = sb("kvb_bf", [32, 1], BF16)
    tl2oT = sb("tl2oT", [32, 64], F32)
    tkv2lw = sb("tkv2lw", [32, 64], F32)
    twqk = sb("twqk", [128, 32], F32)
    wqkb_sb = sb("wqkb_sb", [128, 1], F32)
    b_wqks = sb("b_wqks", [128, 1], F32)
    b_q2lT = sb("b_q2lT", [64, 32], BF16)
    b_kv2lT = sb("b_kv2lT", [64, 32], BF16)
    l2oT_bf = sb("l2oT_bf", [32, 64], BF16)
    kv2lw_bf = sb("kv2lw_bf", [32, 64], BF16)
    wqk4_bf = sb("wqk4_bf", [128, 32], BF16)
    AT_bf = sb("AT_bf", [64, 64], BF16)
    b_qlat = sb("b_qlat", [128, 4], F32)
    b_qlat_r = sb("b_qlat_r", [128, 4], F32)
    b_klat = sb("b_klat", [128, 4], F32)
    b_klat_r = sb("b_klat_r", [128, 4], F32)
    b_vo2 = sb("b_vo2", [128, 1], F32)
    cos_bf = sb("cos_bf", [128, HALF], BF16)
    sin_bf = sb("sin_bf", [128, HALF], BF16)
    causal_sb = sb("causal_sb", [128, 128], BF16)
    id128_sb = sb("id128_sb", [128, 128], BF16)
    id2_sb = sb("id2_sb", [128, 64], BF16)
    ones128_sb = sb("ones128_sb", [128, 1], BF16)
    ones1x_sb = sb("ones1x_sb", [1, 128], BF16)
    onesblk_sb = sb("onesblk_sb", [128, 4], BF16)
    c_eff = sb("c_eff", [128, 8], F32)
    cb_sb = sb("cb_sb", [128, 8], F32)
    tl2o_sb = sb("tl2o_sb", [128, 8], F32)
    tl2o_bf = sb("tl2o_bf", [128, 8], BF16)
    eps_sb = sb("eps_sb", [1, 1], F32)

    disha_sb = ln_sb
    out_stg = stg
    PS = st.enter_context(nc.psum_tensor("PS", [128, 4096], F32))
    PSB = PS.bitcast(BF16)

    sems = {}
    for s in Prog.SEM_NAMES:
        sems[s] = st.enter_context(nc.semaphore(s))

    p = Prog()
    BK = [p.R("bank", i) for i in range(8)]

    def bank_ap(i, lo, hi):
        return PS[:, 512 * i + lo:512 * i + hi]

    slot_ctr = [0]

    def stream_slot(src_ap, src_res):
        j = slot_ctr[0] % NSLOT
        slot_ctr[0] += 1
        r = p.R("slot", j)
        p.op("sync", lambda eng, s=src_ap, j=j: eng.dma_start(
            out=wslot[:, 1024 * j:1024 * j + 1024], in_=s),
            reads=[src_res], writes=[r], dma="SL%d" % j)
        return j, r

    def slot_ap(j, d):
        return wslot[:, 1024 * j + 128 * d:1024 * j + 128 * d + 128]

    def ld(dst_ap, src_ap, res):
        p.op("sync", lambda eng, d=dst_ap, s=src_ap: eng.dma_start(out=d, in_=s),
             writes=[res], dma="LDI")

    for o in range(NCH):
        p.op("sync", lambda eng, o=o: eng.dma_start(
            out=x_sb[:, HALF * o:HALF * o + HALF],
            in_=d_x[0, 128 * o:128 * o + 128, 0:HALF]),
            writes=[p.R("x", o, tt) for tt in range(NT)], dma="LDI")

    ld(ba_sb[:], d_ba48, p.R("ba"))
    ld(tq2lT[:], d_q2lT, p.R("tq2lT"))
    ld(tkv2lT[:], d_kv2lT, p.R("tkv2lT"))
    ld(q2lb_sb[:], d_q2lb, p.R("q2lb"))
    ld(kv2lb_sb[:], d_kv2lb, p.R("kv2lb"))
    ld(tl2oT[:], d_l2oT, p.R("tl2oT"))
    ld(tkv2lw[:], d_kv2lw, p.R("tkv2lw"))
    ld(twqk[:], d_wqkT4, p.R("twqk"))
    ld(wqkb_sb[:], d_wqkb4, p.R("wqkb"))
    ld(cb_sb[:], d_cpb8, p.R("cb"))
    ld(tl2o_sb[:], d_tl2o8, p.R("tl2o"))
    ld(stg[:, 0:1024], d_disha2, p.R("stg"))
    p.op("vector", lambda eng: eng.tensor_copy(disha_sb[:, 0:1024], stg[:, 0:1024]),
         reads=[p.R("stg")], writes=[p.R("disha")])
    ld(causal_sb[:], d_causal, p.R("causal"))
    ld(id128_sb[:], d_id128, p.R("id128"))
    p.op("vector", lambda eng: eng.memset(eps_sb[:], 1e-5), writes=[p.R("eps")])
    ld(id2_sb[:], d_id2, p.R("id2"))
    ld(ones128_sb[:], d_ones128, p.R("ones128"))
    ld(ones1x_sb[:], d_ones1x, p.R("ones1x"))
    ld(onesblk_sb[:], d_onesblk, p.R("onesblk"))

    def cp32(dst_lo, dst_hi, src_lo, src_hi, tensor_d, tensor_s, rres, wres):
        for i in range((dst_hi - dst_lo) // 32):
            p.op("vector", lambda eng, a=dst_lo + 32 * i, b=src_lo + 32 * i,
                 td=tensor_d, tsr=tensor_s: eng.tensor_copy(
                     td[a:a + 32, :], tsr[b:b + 32, :]),
                 reads=[rres], writes=[wres])

    p.op("vector", lambda eng: eng.tensor_copy(ba_bf[:], ba_sb[:]),
         reads=[p.R("ba")], writes=[p.R("babf")])
    p.op("vector", lambda eng: eng.tensor_copy(b_q2lT[:], tq2lT[:]),
         reads=[p.R("tq2lT")], writes=[p.R("q2lT2")])
    p.op("vector", lambda eng: eng.tensor_copy(b_kv2lT[:], tkv2lT[:]),
         reads=[p.R("tkv2lT")], writes=[p.R("kv2lT2")])
    p.op("vector", lambda eng: eng.tensor_copy(l2oT_bf[:], tl2oT[:]),
         reads=[p.R("tl2oT")], writes=[p.R("l2oTbf")])
    p.op("vector", lambda eng: eng.tensor_copy(kv2lw_bf[:], tkv2lw[:]),
         reads=[p.R("tkv2lw")], writes=[p.R("kv2lwbf")])
    p.op("vector", lambda eng: eng.tensor_copy(wqk4_bf[:], twqk[:]),
         reads=[p.R("twqk")], writes=[p.R("wqk4bf")])
    p.op("vector", lambda eng: eng.tensor_copy(kvb_bf[:], kv2lb_sb[:]),
         reads=[p.R("kv2lb")], writes=[p.R("kvbbf")])
    p.op("vector", lambda eng: eng.tensor_copy(tl2o_bf[:], tl2o_sb[:]),
         reads=[p.R("tl2o")], writes=[p.R("tl2obf")])
    p.op("scalar", lambda eng: eng.mul(b_wqks[:], wqkb_sb[:], ISQL),
         reads=[p.R("wqkb")], writes=[p.R("bwqks")])

    p.group("tensor", [lambda eng: eng.matmul(
        bank_ap(1, 0, 64)[0:64, :], kv2lw_bf[:], l2oT_bf[:],
        start=True, stop=True)],
        reads=[p.R("kv2lwbf"), p.R("l2oTbf")], writes=[BK[1]])
    p.op("vector", lambda eng: eng.tensor_copy(AT_bf[:], bank_ap(1, 0, 64)[0:64, :]),
         reads=[BK[1]], writes=[p.R("AT2")])

    p.group("tensor", [lambda eng: eng.matmul(
        bank_ap(2, 0, 1)[0:64, :], l2oT_bf[:], kvb_bf[:], start=True, stop=True)],
        reads=[p.R("l2oTbf"), p.R("kvbbf")], writes=[BK[2]])
    p.op("vector", lambda eng: eng.tensor_copy(b_vo2[0:64, :], bank_ap(2, 0, 1)[0:64, :]),
         reads=[BK[2]], writes=[p.R("bvo")])
    cp32(64, 128, 0, 64, b_vo2, b_vo2, p.R("bvo"), p.R("bvo"))

    if DEBUG_STOP == "init_smalls":
        return _finish(nc, p, sems, st)
    def fold_m(m):
        if m < 8:
            qk = m < 4
            hg = m % 4
            rhsT = b_q2lT if qk else b_kv2lT
            rhs_res = p.R("q2lT2") if qk else p.R("kv2lT2")
            brow = 0 if qk else D
            for hh in range(4):
                r0 = brow + 256 * hg + 64 * hh
                ld(stg[0:64, 0:1024], d_wattn[r0:r0 + 64, :], p.R("stg"))
                p.op("vector", lambda eng: eng.tensor_copy(
                    slab1[:], stg[0:64, 0:1024]),
                    reads=[p.R("stg")], writes=[p.R("slab", 0)])
                fns = []
                for c in range(8):
                    bk = 1 if c < 4 else 2
                    col0 = 512 * bk + 128 * (c % 4)
                    fns.append(lambda eng, hh=hh, c=c, col0=col0, rhsT=rhsT:
                               eng.matmul(
                        PS[:, col0 + 32 * hh:col0 + 32 * hh + 32],
                        slab1[:, 128 * c:128 * c + 128],
                        rhsT[:], start=True, stop=True))
                p.group("tensor", fns,
                        reads=[p.R("slab", 0), rhs_res],
                        writes=[BK[1], BK[2]])
            bdst = b_qlat if qk else b_klat
            bres = p.R("bql") if qk else p.R("bkl")
            pb_sb = q2lb_sb if qk else kv2lb_sb
            pb_res = p.R("q2lb") if qk else p.R("kv2lb")
            for hh in range(4):
                h = 4 * hg + hh
                fl = brow // 64 + h
                p.group("tensor", [lambda eng, fl=fl, rhsT=rhsT: eng.matmul(
                    bank_ap(3, 0, 1)[0:32, :], rhsT[:],
                    ba_bf[:, fl:fl + 1], start=True, stop=True)],
                    reads=[rhs_res, p.R("babf")], writes=[BK[3]])
                p.op("vector", lambda eng, hh=hh, hg=hg, bdst=bdst, pb=pb_sb:
                     eng.tensor_scalar(
                         bdst[32 * hh:32 * hh + 32, hg:hg + 1],
                         bank_ap(3, 0, 1)[0:32, :], pb[:, 0:1], None, ALU.add),
                     reads=[BK[3], pb_res], writes=[bres])
        else:
            hp = m - 8
            for hh in range(2):
                r0 = 2 * D + 128 * hp + 64 * hh
                ld(stg[0:64, 0:1024], d_wattn[r0:r0 + 64, :], p.R("stg"))
                p.op("vector", lambda eng: eng.tensor_copy(
                    slab1[:], stg[0:64, 0:1024]),
                    reads=[p.R("stg")], writes=[p.R("slab", 0)])
                fns = []
                for c in range(8):
                    bk = 1 if c < 4 else 2
                    col0 = 512 * bk + 128 * (c % 4)
                    fns.append(lambda eng, hh=hh, c=c, col0=col0: eng.matmul(
                        PS[:, col0 + 64 * hh:col0 + 64 * hh + 64],
                        slab1[:, 128 * c:128 * c + 128],
                        AT_bf[:], start=True, stop=True))
                p.group("tensor", fns,
                        reads=[p.R("slab", 0), p.R("AT2")],
                        writes=[BK[1], BK[2]])
        for half in range(2):
            bk = 1 + half
            p.op("scalar", lambda eng, half=half, bk=bk: eng.activation(
                build_sb[:, 512 * half:512 * half + 512],
                bank_ap(bk, 0, 512), AF.Copy),
                reads=[BK[bk]], writes=[p.R("build")])
        p.op("sync", lambda eng, m=m: eng.dma_start(
            out=wa_strm[m, :, :], in_=build_sb[:]),
            reads=[p.R("build")], writes=[p.R("wa_strm", m)], dma="LDI")

    nfold = 16 if DEBUG_STOP not in ("init_fold1",) else 1
    for m in range(nfold):
        fold_m(m)
    if DEBUG_STOP in ("init_folds", "init_fold1"):
        return _finish(nc, p, sems, st)

    p.op("vector", lambda eng: eng.stream_shuffle(b_qlat_r[:], b_qlat[:], PAIR_MASK),
         reads=[p.R("bql")], writes=[p.R("bqlr")])
    p.op("vector", lambda eng: eng.stream_shuffle(b_klat_r[:], b_klat[:], PAIR_MASK),
         reads=[p.R("bkl")], writes=[p.R("bklr")])

    if DEBUG_STOP == "init_rope":
        return _finish(nc, p, sems, st)
    for c in range(8):
        ld(stg[:, 0:1024], d_ffnT[128 * c:128 * c + 128, :], p.R("stg"))
        p.op("vector", lambda eng: eng.tensor_tensor(
            build_sb[:], stg[:, 0:1024], disha_sb[:, 0:1024], ALU.add),
            reads=[p.R("stg"), p.R("disha")], writes=[p.R("build")])
        p.op("sync", lambda eng, c=c: eng.dma_start(
            out=me_cmaj[c, :, :], in_=build_sb[:]),
            reads=[p.R("build")], writes=[p.R("me_cmaj", c)], dma="LDI")
    for c in range(8):
        ld(stg[:, 0:1024], d_cpT[128 * c:128 * c + 128, :], p.R("stg"))
        p.op("vector", lambda eng: eng.tensor_copy(build_sb[:], stg[:, 0:1024]),
             reads=[p.R("stg")], writes=[p.R("build")])
        p.op("sync", lambda eng, c=c: eng.dma_start(
            out=cp_cmaj[c, :, :], in_=build_sb[:]),
            reads=[p.R("build")], writes=[p.R("cp_cmaj", c)], dma="LDI")

    if DEBUG_STOP == "init_mecp":
        return _finish(nc, p, sems, st)
    for o in range(8):
        p.op("sync", lambda eng, o=o: eng.dma_start(
            out=me_strm[o, :, :].rearrange("p (c j) -> c p j", c=8),
            in_=me_cmaj[:, :, 128 * o:128 * o + 128]),
            reads=[p.R("me_cmaj", c) for c in range(8)],
            writes=[p.R("me_strm", o)], dma="LDI")
        p.op("sync", lambda eng, o=o: eng.dma_start(
            out=cp_strm[o, :, :].rearrange("p (c j) -> c p j", c=8),
            in_=cp_cmaj[:, :, 128 * o:128 * o + 128]),
            reads=[p.R("cp_cmaj", c) for c in range(8)],
            writes=[p.R("cp_strm", o)], dma="LDI")

    def x_ap(o, hb_local_cols):
        lo, hi = hb_local_cols
        return x_sb[:, HALF * o + lo:HALF * o + hi]

    def ln_stats_chunk(tt, c, sqb=6):
        g0 = TTA * tt
        par = c % 2
        xres = p.R("x", c, tt)
        p.op("gpsimd", lambda eng, c=c, g0=g0: eng.tensor_copy(
            xb_sb[:, TTA * c:TTA * c + TTA], x_ap(c, (g0, g0 + TTA))),
            reads=[xres], writes=[p.R("xb", c)])
        p.op("gpsimd", lambda eng, c=c, par=par: eng.tensor_tensor(
            sq_sb[:, TTA * par:TTA * par + TTA],
            xb_sb[:, TTA * c:TTA * c + TTA],
            xb_sb[:, TTA * c:TTA * c + TTA], ALU.mult),
            reads=[p.R("xb", c)], writes=[p.R("sq", par)])
        p.group("tensor", [lambda eng, c=c: eng.matmul(
            bank_ap(0, 0, TTA)[0:1, :], ones128_sb[:],
            xb_sb[:, TTA * c:TTA * c + TTA],
            start=(c == 0), stop=(c == NCH - 1))],
            reads=[p.R("xb", c), p.R("ones128")], writes=[BK[0]])
        p.group("tensor", [lambda eng, c=c, par=par, sqb=sqb: eng.matmul(
            bank_ap(sqb, 0, TTA)[0:1, :], ones128_sb[:],
            sq_sb[:, TTA * par:TTA * par + TTA],
            start=(c == 0), stop=(c == NCH - 1))],
            reads=[p.R("sq", par), p.R("ones128")], writes=[BK[sqb]])

    def ln_finish(b, hb, tt, sqb=6):
        p.op("scalar", lambda eng: eng.activation(
            m_bf[:], bank_ap(0, 0, TTA)[0:1, :], AF.Copy, scale=1.0 / D),
            reads=[BK[0]], writes=[p.R("m_bf")])
        p.op("scalar", lambda eng: eng.activation(
            msq[:], m_bf[:], AF.Square),
            reads=[p.R("m_bf")], writes=[p.R("msq")])
        p.op("vector", lambda eng, sqb=sqb: eng.scalar_tensor_tensor(
            var_f[:], bank_ap(sqb, 0, TTA)[0:1, :], 1.0 / D, msq[:],
            ALU.mult, ALU.subtract),
            reads=[BK[sqb], p.R("msq")], writes=[p.R("var")])
        p.op("scalar", lambda eng: eng.activation(
            var_f[:], var_f[:], AF.Ln, bias=eps_sb[0:1, 0:1]),
            reads=[p.R("var"), p.R("eps")], writes=[p.R("var")])
        p.op("scalar", lambda eng: eng.activation(
            a_bf[:], var_f[:], AF.Exp, scale=-0.5),
            reads=[p.R("var")], writes=[p.R("a_bf")])
        p.group("tensor", [
            lambda eng: eng.matmul(bank_ap(6, 0, TTA), ones1x_sb[:], m_bf[:],
                                   start=True, stop=True),
            lambda eng: eng.matmul(bank_ap(7, 0, TTA), ones1x_sb[:], a_bf[:],
                                   start=True, stop=True)],
            reads=[p.R("ones1x")], writes=[BK[6], BK[7]],
            fn_reads=[[p.R("m_bf")], [p.R("a_bf")]])
        p.op("vector", lambda eng: eng.tensor_copy(mb_sb[:], bank_ap(6, 0, TTA)),
             reads=[BK[6]], writes=[p.R("mb")])
        p.op("vector", lambda eng: eng.tensor_copy(ab_sb[:], bank_ap(7, 0, TTA)),
             reads=[BK[7]], writes=[p.R("ab")])
        for c in range(NCH):
            p.op("vector", lambda eng, c=c: eng.tensor_tensor(
                ln_sb[:, TTA * c:TTA * c + TTA],
                xb_sb[:, TTA * c:TTA * c + TTA], mb_sb[:], ALU.subtract),
                reads=[p.R("xb", c), p.R("mb")], writes=[p.R("ln", c)])
            p.op("vector", lambda eng, c=c: eng.tensor_tensor(
                ln_sb[:, TTA * c:TTA * c + TTA],
                ln_sb[:, TTA * c:TTA * c + TTA], ab_sb[:], ALU.mult),
                reads=[p.R("ab"), p.R("ln", c)], writes=[p.R("ln", c)])

    ost_ctr = [0]

    def emit_ffn(b, hb, tt, ln_next=None, lnbank=6, store=False, nxt=None):
        g0 = TTA * tt
        for o in range(NCH):
            j, sres = stream_slot(me_strm[o, :, :], p.R("me_strm", o))
            bk = 4 + o % 2
            fns = []
            for dd in range(NCH):
                fns.append(lambda eng, j=j, dd=dd, bk=bk: eng.matmul(
                    bank_ap(bk, 0, TTA), slot_ap(j, dd),
                    ln_sb[:, TTA * dd:TTA * dd + TTA],
                    start=(dd == 0), stop=(dd == NCH - 1)))
            p.group("tensor", fns, reads=[sres], writes=[BK[bk]],
                    fn_reads=[[p.R("ln", dd)] for dd in range(NCH)])
            xr = p.R("x", o, tt)
            if store:
                j2 = ost_ctr[0] % 2
                ost_ctr[0] += 1
                p.op("vector", lambda eng, o=o, g0=g0, bk=bk, j2=j2:
                     eng.scalar_tensor_tensor(
                         out_stg[:, TTA * j2:TTA * j2 + TTA],
                         bank_ap(bk, 0, TTA), 0.5,
                         x_ap(o, (g0, g0 + TTA)), ALU.mult, ALU.add),
                     reads=[BK[bk], xr], writes=[p.R("ostg", j2)])
                p.op("scalar", lambda eng, o=o, g0=g0, b=b, hb=hb, j2=j2:
                     eng.dma_start(
                         out=d_out[b, 128 * o:128 * o + 128,
                                   HALF * hb + g0:HALF * hb + g0 + TTA],
                         in_=out_stg[:, TTA * j2:TTA * j2 + TTA]),
                     reads=[p.R("ostg", j2)],
                     writes=[p.R("outdram", b, hb, o, tt)], dma="ST")
                if nxt is not None:
                    if tt == NT - 1 and o >= 3:
                        ln_stats_chunk(0, o - 3, sqb=3)
                    nb2, nhb2 = nxt
                    p.op("scalar", lambda eng, b2=nb2, hb2=nhb2, o=o, g0=g0:
                         eng.dma_start(
                             out=x_sb[:, HALF * o + g0:HALF * o + g0 + TTA],
                             in_=d_x[b2, 128 * o:128 * o + 128,
                                     HALF * hb2 + g0:HALF * hb2 + g0 + TTA]),
                         writes=[p.R("x", o, tt)],
                         dma="PF")
            else:
                p.op("vector", lambda eng, o=o, g0=g0, bk=bk:
                     eng.scalar_tensor_tensor(
                         x_ap(o, (g0, g0 + TTA)), bank_ap(bk, 0, TTA), 0.5,
                         x_ap(o, (g0, g0 + TTA)), ALU.mult, ALU.add),
                     reads=[BK[bk], xr], writes=[xr])
            if ln_next is not None and o >= 3:
                ln_stats_chunk(ln_next, o - 3, sqb=lnbank)
        if ln_next is not None:
            ln_stats_chunk(ln_next, 5, sqb=lnbank)
            ln_stats_chunk(ln_next, 6, sqb=lnbank)
            ln_stats_chunk(ln_next, 7, sqb=lnbank)
        if store and nxt is not None and tt == NT - 1:
            ln_stats_chunk(0, 5, sqb=3)
            ln_stats_chunk(0, 6, sqb=3)
            ln_stats_chunk(0, 7, sqb=3)

    def emit_cattn(b, hb, tt, ln_next=None):
        g0 = TTA * tt
        gt0 = HALF * hb + g0
        ta = (HALF * hb + g0) // TTA
        pending = []

        def defer(due, fn):
            pending.append((due, fn))

        def flush(m=99):
            while pending and pending[0][0] <= m:
                pending.pop(0)[1]()

        for m in range(16):
            if ln_next is not None and 8 <= m:
                ln_stats_chunk(ln_next, m - 8)
            j, sres = stream_slot(wa_strm[m, :, :], p.R("wa_strm", m))
            bk = 1 + m % 3
            nb = 4 + m % 2
            fns = []
            for dd in range(NCH):
                fns.append(lambda eng, j=j, dd=dd, bk=bk: eng.matmul(
                    bank_ap(bk, 0, TTA), slot_ap(j, dd),
                    ln_sb[:, TTA * dd:TTA * dd + TTA],
                    start=(dd == 0), stop=(dd == NCH - 1)))
            p.group("tensor", fns, reads=[sres], writes=[BK[bk]],
                    fn_reads=[[p.R("ln", dd)] for dd in range(NCH)])
            flush(m)
            if m < 8:
                qk = m < 4
                hg = m % 4
                bias = b_qlat if qk else b_klat
                bias_r = b_qlat_r if qk else b_klat_r
                bias_res = [p.R("bql" if qk else "bkl"),
                            p.R("bqlr" if qk else "bklr")]
                p.op("vector", lambda eng, bk=bk: eng.stream_shuffle(
                    qrot[:], bank_ap(bk, 0, TTA), PAIR_MASK),
                    reads=[BK[bk]], writes=[p.R("qrot")])
                p.op("vector", lambda eng, bk=bk, hg=hg, bias=bias, g0=g0:
                     eng.scalar_tensor_tensor(
                         t1_sb[:], bank_ap(bk, 0, TTA), bias[:, hg:hg + 1],
                         cos_bf[:, g0:g0 + TTA], ALU.add, ALU.mult),
                     reads=[BK[bk], bias_res[0], p.R("cos")], writes=[p.R("t1")])
                p.op("vector", lambda eng, hg=hg, bias_r=bias_r, g0=g0:
                     eng.scalar_tensor_tensor(
                         t2_sb[:], qrot[:], bias_r[:, hg:hg + 1],
                         sin_bf[:, g0:g0 + TTA], ALU.add, ALU.mult),
                     reads=[p.R("qrot"), bias_res[1], p.R("sin")],
                     writes=[p.R("t2")])
                if qk:
                    qpar = m % 2
                    p.op("vector", lambda eng, qpar=qpar: eng.tensor_tensor(
                        qp_sb[:, TTA * qpar:TTA * qpar + TTA],
                        t1_sb[:], t2_sb[:], ALU.add),
                        reads=[p.R("t1"), p.R("t2")], writes=[p.R("qp", qpar)])

                    def q_follow(hg=hg, gt0=gt0, ta=ta, nb=nb, qpar=qpar):
                        fns = []
                        for i in range(4):
                            fns.append(lambda eng, i=i, nb=nb, qpar=qpar:
                                       eng.matmul(
                                bank_ap(nb, 0, TTA)[32 * i:32 * i + 32, :],
                                wqk4_bf[32 * i:32 * i + 32, :],
                                qp_sb[32 * i:32 * i + 32,
                                      TTA * qpar:TTA * qpar + TTA],
                                start=True, stop=True,
                                tile_position=(32 * i, 32 * i)))
                        p.group("tensor", fns,
                                reads=[p.R("qp", qpar), p.R("wqk4bf")],
                                writes=[BK[nb]])
                        qdst = p.R("q", hg, ta)
                        p.op("vector", lambda eng, hg=hg, gt0=gt0, nb=nb:
                             eng.tensor_scalar(
                                 q_store[:, seq * hg + gt0:seq * hg + gt0 + TTA],
                                 bank_ap(nb, 0, TTA), b_wqks[:, 0:1], ISQL,
                                 ALU.add, ALU.mult),
                             reads=[BK[nb], p.R("bwqks")], writes=[qdst])
                        par2 = hg % 2
                        p.op("scalar", lambda eng, nb=nb, par2=par2:
                             eng.activation(
                                 sq2_sb[:, TTA * par2:TTA * par2 + TTA],
                                 bank_ap(nb, 0, TTA), AF.Square, scale=ISQL),
                             reads=[BK[nb]], writes=[p.R("sq2", par2)])

                    def q_norm(hg=hg, ta=ta):
                        par2 = hg % 2
                        p.group("tensor", [lambda eng, par2=par2: eng.matmul(
                            bank_ap(7, 0, TTA)[0:4, :], onesblk_sb[:],
                            sq2_sb[:, TTA * par2:TTA * par2 + TTA],
                            start=True, stop=True)],
                            reads=[p.R("sq2", par2), p.R("onesblk")],
                            writes=[BK[7]])
                        p.op("vector", lambda eng, hg=hg, ta=ta:
                             eng.reduce_max(
                                 qn_acc[:, 4 * ta + hg:4 * ta + hg + 1],
                                 bank_ap(7, 0, TTA)[0:4, :], axis=AX.X),
                             reads=[BK[7]], writes=[p.R("qn", 4 * ta + hg)])
                    defer(m + 2, q_follow)
                    defer(m + 3, q_norm)
                else:
                    kdst = p.R("k", hg, ta)
                    p.op("vector", lambda eng, hg=hg, gt0=gt0: eng.tensor_tensor(
                        k_store[:, seq * hg + gt0:seq * hg + gt0 + TTA],
                        t1_sb[:], t2_sb[:], ALU.add),
                        reads=[p.R("t1"), p.R("t2")], writes=[kdst])

                    def k_follow(hg=hg, gt0=gt0, ta=ta, kdst=kdst):
                        par2 = hg % 2
                        p.op("vector", lambda eng, hg=hg, gt0=gt0, par2=par2:
                             eng.tensor_tensor(
                                 sq2_sb[:, TTA * par2:TTA * par2 + TTA],
                                 k_store[:, seq * hg + gt0:seq * hg + gt0 + TTA],
                                 k_store[:, seq * hg + gt0:seq * hg + gt0 + TTA],
                                 ALU.mult),
                             reads=[kdst], writes=[p.R("sq2", par2)])

                    def k_norm(hg=hg, ta=ta):
                        par2 = hg % 2
                        p.group("tensor", [lambda eng, par2=par2: eng.matmul(
                            bank_ap(7, 0, TTA)[0:4, :], onesblk_sb[:],
                            sq2_sb[:, TTA * par2:TTA * par2 + TTA],
                            start=True, stop=True)],
                            reads=[p.R("sq2", par2), p.R("onesblk")],
                            writes=[BK[7]])
                        p.op("vector", lambda eng, hg=hg, ta=ta:
                             eng.reduce_max(
                                 kn_acc[:, 4 * ta + hg:4 * ta + hg + 1],
                                 bank_ap(7, 0, TTA)[0:4, :], axis=AX.X),
                             reads=[BK[7]], writes=[p.R("kn", 4 * ta + hg)])
                    defer(m + 2, k_follow)
                    defer(m + 3, k_norm)
            else:
                hp = m - 8
                vpar = m % 2
                p.op("vector", lambda eng, bk=bk, vpar=vpar: eng.tensor_scalar(
                    v_sb[:, TTA * vpar:TTA * vpar + TTA],
                    bank_ap(bk, 0, TTA), b_vo2[:, 0:1], None, ALU.add),
                    reads=[BK[bk], p.R("bvo")], writes=[p.R("vsb", vpar)])
                nsub = TTA // 128
                cg0 = (HALF * hb + g0) // 128
                vdst = [p.R("v", cg0 + jj) for jj in range(nsub)]

                def v_follow(hp=hp, nsub=nsub, cg0=cg0, vdst=vdst, vpar=vpar):
                    for hh2 in range(2):
                        base = 5120 - 1024 * hh2
                        fns = []
                        for jj in range(nsub):
                            fns.append(lambda eng, jj=jj, hh2=hh2, vpar=vpar,
                                       base=base: eng.transpose(
                                PSB[:, base + 64 * jj:base + 64 * jj + 64],
                                v_sb[64 * hh2:64 * hh2 + 64,
                                     TTA * vpar + 128 * jj:TTA * vpar + 128 * jj + 128],
                                id2_sb[64 * hh2:64 * hh2 + 64, :]))
                        p.group("tensor", fns,
                                reads=[p.R("vsb", vpar), p.R("id2")],
                                writes=[BK[5 - hh2]])
                        p.op("vector", lambda eng, cg0=cg0, hp=hp, hh2=hh2,
                             nsub=nsub, vst4=vst4, base=base: eng.tensor_copy(
                                 vst4[:, cg0:cg0 + nsub, 2 * hp + hh2, 0:64],
                                 PSB.ap().rearrange("p (j e) -> p j e",
                                                    j=128, e=64)[:, base // 64:base // 64 + nsub, :]),
                             reads=[BK[5 - hh2]], writes=vdst)
                defer(m + 2, v_follow)
        flush()

    def emit_bexp(b, hb):
        ncol = 4 * (2 * hb + 2) if TTA == 512 else 4 * NT * (hb + 1)
        qres = [p.R("qn", i) for i in range(ncol)]
        kres = [p.R("kn", i) for i in range(ncol)]
        p.op("vector", lambda eng, ncol=ncol: eng.reduce_max(
            qmx[:], qn_acc[:, 0:ncol], axis=AX.X), reads=qres, writes=[p.R("qmx")])
        p.op("vector", lambda eng, ncol=ncol: eng.reduce_max(
            kmx[:], kn_acc[:, 0:ncol], axis=AX.X), reads=kres, writes=[p.R("kmx")])
        p.op("vector", lambda eng: eng.tensor_tensor(
            prod4[:], qmx[:], kmx[:], ALU.mult),
            reads=[p.R("qmx"), p.R("kmx")], writes=[p.R("prod4")])
        p.op("sync", lambda eng: eng.dma_start(out=g14[:], in_=prod4[:]),
             reads=[p.R("prod4")], writes=[p.R("g14")], dma="LDI")
        p.op("vector", lambda eng: eng.reduce_max(mmax[:], g14[:], axis=AX.X),
             reads=[p.R("g14")], writes=[p.R("mmax")])
        p.op("scalar", lambda eng: eng.activation(msqrt[:], mmax[:], AF.Sqrt),
             reads=[p.R("mmax")], writes=[p.R("msqrt")])
        p.op("vector", lambda eng: eng.tensor_single_scalar(
            mneg_bf[:], msqrt[:], -1.0, ALU.mult),
            reads=[p.R("msqrt")], writes=[p.R("mneg")])
        p.group("tensor", [lambda eng: eng.matmul(
            bank_ap(0, 0, 1), ones1x_sb[:], mneg_bf[:], start=True, stop=True)],
            reads=[p.R("mneg"), p.R("ones1x")], writes=[BK[0]])
        p.op("scalar", lambda eng: eng.activation(
            b_exp[:], bank_ap(0, 0, 1), AF.Copy),
            reads=[BK[0]], writes=[p.R("bexp")])

    def emit_attn_pair(b, hb, h0, defer_in=None):
        nchunks = (hb + 1) * CH
        regions = [(0, min(512, QW))] + ([(512, QW)] if QW > 512 else [])
        lastc = {}
        for (lo, hi) in regions:
            lastc[(lo, hi)] = max(c for c in range(nchunks)
                                  if max(0, 128 * c - HALF * hb) < hi)

        def s_chunk(h, c):
            hg, gq = h % 4, h // 4
            par = h % 2
            qoff = max(0, 128 * c - HALF * hb)
            ta_k = (128 * c) // TTA
            kread = p.R("k", gq, ta_k)
            qread = [p.R("q", gq, (HALF * hb) // TTA + t) for t in range(NT)]
            for (lo, hi) in regions:
                a, bnd = max(lo, qoff), hi
                if a >= bnd:
                    continue
                bkidx = 2 * par + lo // 512
                p.group("tensor", [lambda eng, a=a, bnd=bnd, par=par, c=c,
                                   hg=hg, gq=gq: eng.matmul(
                    PS[:, 1024 * par + a:1024 * par + bnd],
                    k_store[32 * hg:32 * hg + 32,
                            seq * gq + 128 * c:seq * gq + 128 * c + 128],
                    q_store[32 * hg:32 * hg + 32,
                            seq * gq + HALF * hb + a:seq * gq + HALF * hb + bnd],
                    start=True, stop=True, tile_position=(32 * hg, 0))],
                    reads=[kread] + qread, writes=[BK[bkidx]])
            slot = 2 * par + c % 2
            rbanks = [BK[2 * par + i] for i in ((0, 1) if (qoff < 512 and QW > 512)
                                               else ((0,) if qoff < 512 else (1,)))]
            p.op("scalar", lambda eng, slot=slot, par=par, qoff=qoff:
                 eng.activation(
                     PT[:, QW * slot + qoff:QW * slot + QW],
                     PS[:, 1024 * par + qoff:1024 * par + QW], AF.Exp,
                     bias=b_exp[:, 0:1]),
                 reads=rbanks + [p.R("bexp")], writes=[p.R("pt", slot)])
            if 128 * c >= HALF * hb:
                p.op("vector", lambda eng, slot=slot, qoff=qoff: eng.tensor_tensor(
                    PT[:, QW * slot + qoff:QW * slot + qoff + 128],
                    PT[:, QW * slot + qoff:QW * slot + qoff + 128],
                    causal_sb[:], ALU.mult),
                    reads=[p.R("causal"), p.R("pt", slot)], writes=[p.R("pt", slot)])

        def pv_chunk(h, c):
            par = h % 2
            slot = 2 * par + c % 2
            qoff = max(0, 128 * c - HALF * hb)
            yb = 2048 + 1024 * par
            vread = [p.R("v", c), p.R("vones")]
            for (lo, hi) in regions:
                a, bnd = max(lo, qoff), hi
                if a >= bnd:
                    continue
                startv = (c == 0)
                stopv = (c == lastc[(lo, hi)])
                wb = [BK[4 + 2 * par]] if a < 512 else [BK[5 + 2 * par]]
                p.group("tensor", [lambda eng, a=a, bnd=bnd, slot=slot,
                                   c=c, h=h, startv=startv, stopv=stopv,
                                   yb=yb: eng.matmul(
                    PS[0:65, yb + a:yb + bnd],
                    v_store[:, 1040 * c + 65 * h:1040 * c + 65 * h + 65],
                    PT[:, QW * slot + a:QW * slot + bnd],
                    start=startv, stop=stopv)],
                    reads=vread + [p.R("pt", slot)],
                    writes=wb)

        for c in range(nchunks):
            s_chunk(h0, c)
            s_chunk(h0 + 1, c)
            if c == min(1, nchunks - 1) and defer_in is not None:
                defer_in()
                defer_in = None
            if c >= 1:
                pv_chunk(h0, c - 1)
                pv_chunk(h0 + 1, c - 1)
        pv_chunk(h0, nchunks - 1)
        pv_chunk(h0 + 1, nchunks - 1)

        yball = [BK[4], BK[6]] + ([BK[5], BK[7]] if QW > 512 else [])
        p.op("scalar", lambda eng: eng.activation(
            r_sb.ap().rearrange("p (s w) -> p s w", w=QW)[0:1, 0:2, :],
            PS.ap().rearrange("p (s w) -> p s w", w=1024)[64:65, 2:4, 0:QW],
            AF.Ln),
            reads=yball, writes=[p.R("r")])
        p.op("scalar", lambda eng: eng.activation(
            r_bf[:, 0:2 * QW], r_sb[:, 0:2 * QW], AF.Exp, scale=-1.0),
            reads=[p.R("r")], writes=[p.R("rbf", 0), p.R("rbf", 1)])

        def mk_epilogue(h):
            par = h % 2
            yb = 2048 + 1024 * par
            ybanks = [BK[4 + 2 * par]] + ([BK[5 + 2 * par]] if QW > 512 else [])

            def epilogue(h=h, yb=yb, ybanks=ybanks, par=par):
                p.group("tensor", [lambda eng, yb=yb, par=par: eng.matmul(
                    PS[64:128, yb:yb + min(512, QW)], ones1x_sb[0:1, 0:64],
                    r_bf[:, QW * par:QW * par + min(512, QW)],
                    start=True, stop=True)] +
                    ([lambda eng, yb=yb, par=par: eng.matmul(
                        PS[64:128, yb + 512:yb + QW], ones1x_sb[0:1, 0:64],
                        r_bf[:, QW * par + 512:QW * par + QW],
                        start=True, stop=True)] if QW > 512 else []),
                    reads=[p.R("rbf", par), p.R("ones1x")],
                    writes=ybanks)
                p.op("vector", lambda eng, yb=yb, par=par: eng.tensor_copy(
                    rb_sb[:, QW * par:QW * par + QW], PS[64:128, yb:yb + QW]),
                    reads=ybanks, writes=[p.R("rb", par)])
                hp, sub = h // 2, h % 2
                p.op("vector", lambda eng, hp=hp, sub=sub, yb=yb, par=par:
                     eng.tensor_tensor(
                         y_sb[64 * sub:64 * sub + 64, QW * hp:QW * hp + QW],
                         PS[0:64, yb:yb + QW], rb_sb[:, QW * par:QW * par + QW],
                         ALU.mult),
                     reads=ybanks + [p.R("rb", par)], writes=[p.R("ysb", hp, sub)])
            return epilogue

        ep0 = mk_epilogue(h0)
        ep1 = mk_epilogue(h0 + 1)

        def both():
            ep0()
            ep1()
        return both

    ceff_done = [False]

    def emit_cproj(b, hb):
        gctr = [0]
        segs = [(0, min(512, QW))] + ([(512, QW)] if QW > 512 else [])
        for si, (lo, hi) in enumerate(segs):
            for o in range(NCH):
                j, sres = stream_slot(cp_strm[o, :, :], p.R("cp_strm", o))
                if not ceff_done[0]:
                    bk = 6 + gctr[0] % 2
                    gctr[0] += 1
                    fns = []
                    for dd in range(NCH):
                        fns.append(lambda eng, j=j, dd=dd, bk=bk: eng.matmul(
                            bank_ap(bk, 0, 1), slot_ap(j, dd),
                            tl2o_bf[:, dd:dd + 1],
                            start=(dd == 0), stop=(dd == NCH - 1)))
                    p.group("tensor", fns, reads=[sres, p.R("tl2obf")],
                            writes=[BK[bk]])
                    p.op("vector", lambda eng, o=o, bk=bk: eng.tensor_tensor(
                        c_eff[:, o:o + 1], bank_ap(bk, 0, 1), cb_sb[:, o:o + 1],
                        ALU.add),
                        reads=[BK[bk], p.R("cb")], writes=[p.R("ceff", o)])
                bk = 6 + gctr[0] % 2
                gctr[0] += 1
                fns = []
                for dd in range(NCH):
                    fns.append(lambda eng, j=j, dd=dd, bk=bk, lo=lo, hi=hi:
                               eng.matmul(
                        bank_ap(bk, 0, hi - lo), slot_ap(j, dd),
                        y_sb[:, QW * dd + lo:QW * dd + hi],
                        start=(dd == 0), stop=(dd == NCH - 1)))
                p.group("tensor", fns, reads=[sres], writes=[BK[bk]],
                        fn_reads=[[p.R("ysb", dd, 0), p.R("ysb", dd, 1)]
                                  for dd in range(NCH)])
                tt = lo // TTA
                xr = p.R("x", o, tt)
                p.op("vector", lambda eng, o=o, lo=lo, hi=hi, bk=bk:
                     eng.scalar_tensor_tensor(
                         x_ap(o, (lo, hi)), bank_ap(bk, 0, hi - lo),
                         c_eff[:, o:o + 1], x_ap(o, (lo, hi)),
                         ALU.add, ALU.add),
                     reads=[BK[bk], xr, p.R("ceff", o)], writes=[xr])
                if si == len(segs) - 1 and o >= 3:
                    ln_stats_chunk(0, o - 3, sqb=2)
            if si == len(segs) - 1:
                ln_stats_chunk(0, 5, sqb=2)
                ln_stats_chunk(0, 6, sqb=2)
                ln_stats_chunk(0, 7, sqb=2)
        ceff_done[0] = True

    def emit_store(b, hb):
        for o in range(NCH):
            xres = [p.R("x", o, tt) for tt in range(NT)]
            p.op("scalar", lambda eng, o=o, b=b, hb=hb: eng.dma_start(
                out=d_out[b, 128 * o:128 * o + 128, HALF * hb:HALF * hb + HALF],
                in_=x_sb[:, HALF * o:HALF * o + HALF]),
                reads=xres, writes=[p.R("outdram", b, hb, o)], dma="ST")

    vst4 = v_store.ap().rearrange("p (c h e) -> p c h e", c=CHB, h=16, e=65)
    stop = DEBUG_STOP
    prefetched = [False]
    for b in range(bpc):
        if stop == "init":
            break
        p.op("vector", lambda eng: eng.memset(vst4[:, :, :, 64:65], 1.0),
             reads=[], writes=[p.R("vones")])
        for hb in range(2):
            for nm, dsrc, ddst in (("cos", d_cos4, cos_bf),
                                   ("sin", d_sin4, sin_bf)):
                ld(ddst[:], dsrc[:, HALF * hb:HALF * hb + HALF], p.R(nm))
            pf = prefetched[0]
            prefetched[0] = False
            if not pf and not (b == 0 and hb == 0):
                for o in range(NCH):
                    p.op("sync", lambda eng, b=b, hb=hb, o=o: eng.dma_start(
                        out=x_sb[:, HALF * o:HALF * o + HALF],
                        in_=d_x[b, 128 * o:128 * o + 128,
                                HALF * hb:HALF * hb + HALF]),
                        writes=[p.R("x", o, tt) for tt in range(NT)],
                        dma="LDI")
            if stop != "xload":
                if pf:
                    ln_finish(b, hb, 0, sqb=3)
                else:
                    for c in range(NCH):
                        ln_stats_chunk(0, c)
                    ln_finish(b, hb, 0)
                for tt in range(NT):
                    if stop == "ln1":
                        break
                    emit_ffn(b, hb, tt, ln_next=tt)
                    if stop == "ffn1":
                        break
                    ln_finish(b, hb, tt)
                    emit_cattn(b, hb, tt,
                               ln_next=(tt + 1 if tt + 1 < NT else None))
                    if tt + 1 < NT:
                        ln_finish(b, hb, tt + 1)
                if stop in ("ln1", "ffn1", "cattn"):
                    emit_store(b, hb)
                    break
                emit_bexp(b, hb)
                if stop == "bexp":
                    emit_store(b, hb)
                    break
                epi = None
                for hp0 in range(NHEAD // 2):
                    epi = emit_attn_pair(b, hb, 2 * hp0, defer_in=epi)
                if epi is not None:
                    epi()
                    epi = None
                if stop == "heads":
                    emit_store(b, hb)
                    break
                emit_cproj(b, hb)
                if stop == "cproj":
                    emit_store(b, hb)
                    break
                ln_finish(b, hb, 0, sqb=2)
                nxt = (b, 1) if hb == 0 else ((b + 1, 0) if b + 1 < bpc else None)
                emit_ffn(b, hb, 0, ln_next=(1 if NT > 1 else None), lnbank=2,
                         store=True, nxt=nxt)
                if NT > 1:
                    ln_finish(b, hb, 1, sqb=2)
                    emit_ffn(b, hb, 1, store=True, nxt=nxt)
                prefetched[0] = nxt is not None
            else:
                emit_store(b, hb)
        else:
            continue
        break

    for s in ("ST0", "ST1", "ST2", "ST3", "ST4", "ST5", "ST6", "ST7"):
        p.raw_wait("scalar", s, p.cnt[s])
        p.raw_wait("sync", s, p.cnt[s])

    def replay(stream):
        def body(eng):
            for e in stream:
                if e[0] == "w":
                    eng.wait_ge(sems[e[1]], e[2])
                else:
                    ins = e[1](eng)
                    if e[2] is not None:
                        ins.then_inc(sems[e[2]], e[3])
        return body

    with nc.Block() as blk:
        blk.sync(replay(p.streams["sync"]))
        blk.vector(replay(p.streams["vector"]))
        blk.tensor(replay(p.streams["tensor"]))
        blk.scalar(replay(p.streams["scalar"]))
        blk.gpsimd(replay(p.streams["gpsimd"]))

    st.__exit__(None, None, None)
    return nc


def _prep_shared(inputs, seq):
    W = np.ascontiguousarray(inputs["L0_attn_c_attn_w"].astype(np.float32))
    consts = _host_consts(seq)
    sh = {
        "w_attn": W,
        "b_attn48": np.ascontiguousarray(
            inputs["L0_attn_c_attn_b"].astype(np.float32).reshape(48, 64).T),
        "q2l_wT": np.ascontiguousarray(inputs["L0_attn_q2l_w"].astype(np.float32).T),
        "q2l_b": inputs["L0_attn_q2l_b"].astype(np.float32).reshape(32, 1),
        "kv2l_w": np.ascontiguousarray(inputs["L0_attn_kv2l_w"].astype(np.float32)),
        "kv2l_wT": np.ascontiguousarray(inputs["L0_attn_kv2l_w"].astype(np.float32).T),
        "kv2l_b": inputs["L0_attn_kv2l_b"].astype(np.float32).reshape(32, 1),
        "l2o_wT": np.ascontiguousarray(inputs["L0_attn_l2o_w"].astype(np.float32).T),
        "tl2o8": np.ascontiguousarray(
            np.tile(inputs["L0_attn_l2o_b"].astype(np.float32), 16).reshape(8, 128).T),
        "wqk_wT4": np.ascontiguousarray(
            np.tile(inputs["L0_attn_wqk_w"].astype(np.float32).T, (4, 1))),
        "wqk_b4": np.ascontiguousarray(
            np.tile(inputs["L0_attn_wqk_b"].astype(np.float32), 4).reshape(128, 1)),
        "cprojT": np.ascontiguousarray(inputs["L0_attn_cproj_w"].astype(np.float32).T),
        "cproj_b8": np.ascontiguousarray(
            inputs["L0_attn_cproj_b"].astype(np.float32).reshape(8, 128).T),
        "ffnT": np.ascontiguousarray(inputs["L0_ffn_weight"].astype(np.float32).T),
        "disha2": np.ascontiguousarray(
            np.concatenate([inputs["L0_ffn_disha"].astype(np.float32)] * 2, axis=0)),
        "cos4": consts["cos4"], "sin4": consts["sin4"],
        "causal": consts["causal"], "id2": consts["id2"],
        "id128": consts["id128"], "maskbig": consts["maskbig"],
        "ones128": consts["ones128"], "ones1x": consts["ones1x"],
        "onesblk": consts["onesblk"],
    }
    return sh


def run(inputs, bpc, seq, n_cores):
    x = np.asarray(inputs["L0_attn_x"], dtype=np.float32)
    B = x.shape[0]
    assert B == bpc * n_cores
    sh = _prep_shared(inputs, seq)
    nc = build_program(bpc, seq)
    in_maps = []
    for c in range(n_cores):
        m = dict(sh)
        m["x"] = np.ascontiguousarray(
            x[bpc * c:bpc * (c + 1)].transpose(0, 2, 1))
        in_maps.append(m)
    res = run_bass_kernel_spmd(nc, in_maps, core_ids=list(range(n_cores)))
    outs = [np.asarray(r["out"]) for r in res.results]
    out = np.concatenate(outs, axis=0)
    return np.ascontiguousarray(out.transpose(0, 2, 1))


def kernel(**inputs):
    return run(inputs, bpc=4, seq=2048, n_cores=8)
